# version 12
# baseline (speedup 1.0000x reference)
import numpy as np
import ml_dtypes
import concourse.bass as bass
import concourse.mybir as mybir
from concourse.bass_types import AP

F32 = mybir.dt.float32
BF16 = mybir.dt.bfloat16
I32 = mybir.dt.int32
AF = mybir.ActivationFunctionType
ALU = mybir.AluOpType
AX = mybir.AxisListType

NCORES = 8
R = 32
E = 8
BW = 32
EPS_LN = 1e-5
EPS_G = 1e-6
GAMMA = 0.5
SCALING = 0.5


class Ctr:
    def __init__(self, sem):
        self.sem = sem
        self.n = 0

    def inc(self, instr, k=1):
        instr.then_inc(self.sem, k)
        self.n += k
        return self.n


def build_program(TPC, D, detect_races=True):
    NT = TPC // 128
    KC = D // 128
    KC2 = KC // 2
    NSL = D // NCORES
    KQ = D // 4
    KCQ = KC // 4
    NTL = NSL // 128
    KE = KQ // 2
    KBE = KE // BW
    AN = TPC // 2
    NA = AN // 128
    NH = 4 * NTL
    assert TPC % 256 == 0 and D % 256 == 0 and NSL % 128 == 0

    nc = bass.Bass("TRN2", target_bir_lowering=False,
                   detect_race_conditions=detect_races)

    x_in = nc.dram_tensor("x", [TPC, D], F32, kind="ExternalInput")
    mask_sl = nc.dram_tensor("mask_sl", [NSL, D], I32, kind="ExternalInput")
    w_sl = nc.dram_tensor("w_sl", [NSL, D], F32, kind="ExternalInput")
    Wf_sl = nc.dram_tensor("Wf_sl", [NSL, D], F32, kind="ExternalInput")
    W8_in = nc.dram_tensor("W8", [E, D], F32, kind="ExternalInput")
    biasrow_in = nc.dram_tensor("biasrow", [1, D], F32, kind="ExternalInput")
    brow_in = nc.dram_tensor("brow", [1, D], F32, kind="ExternalInput")
    A_in = nc.dram_tensor("A", [D, R], F32, kind="ExternalInput")
    Bmat_in = nc.dram_tensor("Bmat", [R, D], F32, kind="ExternalInput")
    LiMEs_in = nc.dram_tensor("LiMEs", [E, R], F32, kind="ExternalInput")
    identb_in = nc.dram_tensor("identb", [128, 128], BF16, kind="ExternalInput")
    identf_in = nc.dram_tensor("identf", [128, 128], F32, kind="ExternalInput")
    E32_in = nc.dram_tensor("E32", [128, 4], BF16, kind="ExternalInput")
    Emat_in = nc.dram_tensor("Emat", [4, 128], BF16, kind="ExternalInput")
    out_d = nc.dram_tensor("out", [TPC, D], F32, kind="ExternalOutput")

    ag_in1 = nc.dram_tensor("ag_in1", [128, KC2, NSL], BF16)
    ag_in2 = nc.dram_tensor("ag_in2", [128, KC2, NSL], BF16)
    ag_out1 = nc.dram_tensor("ag_out1", [NCORES, 128, KC2, NSL], BF16,
                             addr_space="Shared")
    ag_out2 = nc.dram_tensor("ag_out2", [NCORES, 128, KC2, NSL], BF16,
                             addr_space="Shared")
    ar_in = nc.dram_tensor("ar_in", [1, 8], F32)
    ar_out = nc.dram_tensor("ar_out", [1, 8], F32)
    RG = [list(range(NCORES))]

    x3 = x_in[:, :].rearrange("(t p) d -> p t d", p=128)
    out3 = out_d[:, :].rearrange("(t p) d -> p t d", p=128)

    from contextlib import ExitStack
    ES = ExitStack()
    if True:
        xnT = ES.enter_context(nc.sbuf_tensor("xnT", [128, KC, TPC], BF16))
        AW8b = ES.enter_context(nc.sbuf_tensor("AW8b", [128, KC, R + E], BF16))
        zT_sb = ES.enter_context(nc.sbuf_tensor("zT_sb", [R, TPC], BF16))
        H8T_sb = ES.enter_context(nc.sbuf_tensor("H8T_sb", [E, TPC], BF16))
        dT_sb = ES.enter_context(nc.sbuf_tensor("dT_sb", [E, AN], BF16))
        ltb = ES.enter_context(nc.sbuf_tensor("ltb", [E, AN], BF16))
        wfT = ES.enter_context(nc.sbuf_tensor("wfT", [E, TPC], BF16))
        pm_sb = ES.enter_context(nc.sbuf_tensor("pm_sb", [R, TPC], BF16))
        zpT = ES.enter_context(nc.sbuf_tensor("zpT", [R + 1, TPC], BF16))
        Bextb = ES.enter_context(nc.sbuf_tensor("Bextb", [R + 1, D], BF16))
        b8ext = ES.enter_context(nc.sbuf_tensor("b8ext", [1, R + E], BF16))
        ones128b = ES.enter_context(nc.sbuf_tensor("ones128b", [1, 128], BF16))
        ones8f = ES.enter_context(nc.sbuf_tensor("ones8f", [1, E], F32))
        LiMEsb = ES.enter_context(nc.sbuf_tensor("LiMEsb", [E, R], BF16))
        LT_sb = ES.enter_context(nc.sbuf_tensor("LT_sb", [R, E], BF16))
        identb = ES.enter_context(nc.sbuf_tensor("identb_s", [128, 128], BF16))
        identf = ES.enter_context(nc.sbuf_tensor("identf_s", [128, 128], F32))
        E32b = ES.enter_context(nc.sbuf_tensor("E32b_s", [128, 4], BF16))
        Ematb = ES.enter_context(nc.sbuf_tensor("Ematb", [4, 128], BF16))
        st = ES.enter_context(nc.sbuf_tensor("stats", [128, 8], F32))
        epsc = ES.enter_context(nc.sbuf_tensor("epsc", [128, 1], F32))
        mx2 = ES.enter_context(nc.sbuf_tensor("mx2", [E, 2], F32))
        m21 = ES.enter_context(nc.sbuf_tensor("m21", [2, 1], F32))
        row12 = ES.enter_context(nc.sbuf_tensor("row12", [1, 2], F32))
        arst = ES.enter_context(nc.sbuf_tensor("arst", [1, 8], F32))
        rHd = ES.enter_context(nc.sbuf_tensor("rHd", [E, 2], F32))
        garb = ES.enter_context(nc.sbuf_tensor("garb", [128, 84], F32))
        gbf = ES.enter_context(nc.sbuf_tensor("gbf", [128, 32], BF16))
        wnT_sb = ES.enter_context(nc.sbuf_tensor("wnT_sb", [E, 128], BF16))
        W8b = ES.enter_context(nc.sbuf_tensor("W8b", [E, D], BF16))
        xf = ES.enter_context(nc.sbuf_tensor("xf", [128, 2, D], F32))
        xnb = ES.enter_context(nc.sbuf_tensor("xnb", [128, D], BF16))
        maskb = ES.enter_context(nc.sbuf_tensor("maskb", [128, 2, KQ], I32))
        wb = ES.enter_context(nc.sbuf_tensor("wb", [128, 2, KQ], F32))
        Wbb = ES.enter_context(nc.sbuf_tensor("Wbb", [128, 2, KQ], F32))
        red1w = ES.enter_context(nc.sbuf_tensor("red1w", [128, D // BW // 4 * 4], BF16))
        bm4T = ES.enter_context(nc.sbuf_tensor("bm4T", [4, D // BW], BF16))
        bmex_all = ES.enter_context(nc.sbuf_tensor("bmex_all", [128, NTL, D // BW], F32))
        Mtb = ES.enter_context(nc.sbuf_tensor("Mtb", [128, KQ], BF16))
        MTs = ES.enter_context(nc.sbuf_tensor("MTs", [128, KC, NSL], BF16))
        xr = ES.enter_context(nc.sbuf_tensor("xr", [128, 2, NSL], F32))
        stg = ES.enter_context(nc.sbuf_tensor("stg", [128, 3, NSL], F32))
        ppA = ES.enter_context(nc.psum_tensor("ppA", [128, 512], F32))
        ppB = ES.enter_context(nc.psum_tensor("ppB", [128, 512], F32))
        zhA = ES.enter_context(nc.psum_tensor("zhA", [R + E, 512], F32))
        zhB = ES.enter_context(nc.psum_tensor("zhB", [R + E, 512], F32))
        ptA = ES.enter_context(nc.psum_tensor("ptA", [128, 1024], BF16))
        ptB = ES.enter_context(nc.psum_tensor("ptB", [128, 1024], BF16))
        pfA = ES.enter_context(nc.psum_tensor("pfA", [128, 512], F32))
        pfB = ES.enter_context(nc.psum_tensor("pfB", [128, 512], F32))
        _qA = ES.enter_context(nc.semaphore("qA"))
        _qG = ES.enter_context(nc.semaphore("qG"))
        _qB = ES.enter_context(nc.semaphore("qB"))
        _qC = ES.enter_context(nc.semaphore("qC"))
        _qD = ES.enter_context(nc.semaphore("qD"))
        _cT = ES.enter_context(nc.semaphore("cT"))
        _cV = ES.enter_context(nc.semaphore("cV"))
        _cS = ES.enter_context(nc.semaphore("cS"))
        _cG = ES.enter_context(nc.semaphore("cG"))
        _cc = ES.enter_context(nc.semaphore("cc"))
        _qX0 = ES.enter_context(nc.semaphore("qX0"))
        _qX1 = ES.enter_context(nc.semaphore("qX1"))
        _qM0 = ES.enter_context(nc.semaphore("qM0"))
        _qM1 = ES.enter_context(nc.semaphore("qM1"))
        _qM2 = ES.enter_context(nc.semaphore("qM2"))
        _qM3 = ES.enter_context(nc.semaphore("qM3"))
        _qR0 = ES.enter_context(nc.semaphore("qR0"))
        _qR1 = ES.enter_context(nc.semaphore("qR1"))
        _qW0 = ES.enter_context(nc.semaphore("qW0"))
        _qW1 = ES.enter_context(nc.semaphore("qW1"))
        _qE0 = ES.enter_context(nc.semaphore("qE0"))
        _qE1 = ES.enter_context(nc.semaphore("qE1"))
        _qF = ES.enter_context(nc.semaphore("qF"))
        _s0 = ES.enter_context(nc.semaphore("s0"))
        _s1 = ES.enter_context(nc.semaphore("s1"))
        _s2 = ES.enter_context(nc.semaphore("s2"))
        blk = ES.enter_context(nc.Block())
    with ES:
        qA, qG, qB = Ctr(_qA), Ctr(_qG), Ctr(_qB)
        qC, qD = Ctr(_qC), Ctr(_qD)
        cT, cV, cS, cG, cc = Ctr(_cT), Ctr(_cV), Ctr(_cS), Ctr(_cG), Ctr(_cc)
        qX = [Ctr(_qX0), Ctr(_qX1)]
        qM = [Ctr(_qM0), Ctr(_qM1)]
        qM2 = [Ctr(_qM2), Ctr(_qM3)]
        qR = [Ctr(_qR0), Ctr(_qR1)]
        qW = [Ctr(_qW0), Ctr(_qW1)]
        qE = [Ctr(_qE0), Ctr(_qE1)]
        qF = Ctr(_qF)
        sS = [Ctr(_s0), Ctr(_s1), Ctr(_s2)]
        pp = [ppA, ppB, pfA, pfB]
        zh = [zhA, zhB]
        pt = [ptA, ptB]
        ev = {}

        def done(name, ctr):
            ev[name] = (ctr, ctr.n)

        def wait(eng, name):
            c, v = ev[name]
            if v > 0:
                eng.wait_ge(c.sem, v)

        def vmark(v, name):
            cV.inc(v.drain())
            done(name, cV)

        def smark(s, name):
            cS.inc(s.drain())
            done(name, cS)

        tpar = [0]

        def emit_tgroup(srcs, dst_ap, copy_engine, name, pre_waits=()):
            par = tpar[0] % 2
            tpar[0] += 1
            prev = ev.get(f"ptfree{par}")
            w_out = srcs[0].shape[1]

            def _t(t):
                for wn_ in pre_waits:
                    wait(t, wn_)
                if prev is not None and prev[1] > 0:
                    t.wait_ge(prev[0].sem, prev[1])
                off = 0
                for i, sap in enumerate(srcs):
                    p_in = sap.shape[0]
                    ins = t.transpose(pt[par][0:w_out, off:off + p_in], sap,
                                      identb[0:p_in, 0:p_in])
                    if i == len(srcs) - 1:
                        cT.inc(ins)
                    off += p_in
                done(f"tp_{name}", cT)
            blk.tensor(_t)

            def _c(c):
                wait(c, f"tp_{name}")
                total = sum(sap.shape[0] for sap in srcs)
                srcp = pt[par][0:w_out, 0:total]
                if dst_ap.ndim == 3:
                    srcp = srcp.rearrange("p (a b) -> p a b", b=dst_ap.shape[2])
                if copy_engine == "v":
                    cV.inc(c.tensor_copy(dst_ap, srcp))
                    done(f"ptfree{par}", cV)
                    done(f"cp_{name}", cV)
                else:
                    cS.inc(c.copy(dst_ap, srcp))
                    done(f"ptfree{par}", cS)
                    done(f"cp_{name}", cS)
            (blk.vector if copy_engine == "v" else blk.scalar)(_c)

        def _(g):
            qG.inc(g.dma_start(out=AW8b[:, :, 0:R],
                               in_=A_in[:, :].rearrange("(c k) r -> k c r", k=128)), 16)
            qG.inc(g.dma_start(out=W8b[:, :], in_=W8_in[:, :]), 16)
            qG.inc(g.dma_start(out=LiMEsb[:, :], in_=LiMEs_in[:, :]), 16)

            qG.inc(g.dma_start(out=Bextb[0:R, :], in_=Bmat_in[:, :]), 16)
            qG.inc(g.dma_start(out=identb[:, :], in_=identb_in[:, :]), 16)
            qG.inc(g.dma_start(out=identf[:, :], in_=identf_in[:, :]), 16)
            qG.inc(g.dma_start(out=E32b[:, :], in_=E32_in[:, :]), 16)
            qG.inc(g.dma_start(out=Ematb[:, :], in_=Emat_in[:, :]), 16)
            for nm in ("smalls", "consts"):
                done(nm, qG)
            qB.inc(g.dma_start(out=Bextb[R:R + 1, :], in_=brow_in[:, :]), 16)
            g.wait_ge(_qB, 16)
            qB.inc(g.dma_start(out=Bextb[R:R + 1, :], in_=biasrow_in[:, :],
                               accum_op=ALU.add), 16)
            cG.inc(g.memset(ones128b[:, :], 1.0))
            cG.inc(g.memset(ones8f[:, :], 1.0))
            cG.inc(g.memset(b8ext[:, :], 0.0))
            cG.inc(g.memset(zpT[R:R + 1, :], 1.0))
            cG.inc(g.memset(epsc[:, :], EPS_LN))
            done("memsets", cG)
            g.wait_ge(_cG, cG.n)
            qF.inc(g.dma_start(out=b8ext[0:1, R:R + E], in_=brow_in[0:1, 0:E],
                               allow_slow_non_contiguous=True), 16)
            done("b8l", qF)
        blk.gpsimd(_)
        ev["Bext_row"] = (qB, 32)

        ev["sconsts_v"] = ev["b8l"]

        def _(s):
            wait(s, "smalls")
            cS.inc(s.mul(Bextb[0:R, :], Bextb[0:R, :], SCALING))
            smark(s, "W8b_Bext")
        blk.scalar(_)

        for g8 in range(0, KC, 8):
            kcn = min(8, KC - g8)
            srcs = [W8b[:, kc * 128:(kc + 1) * 128] for kc in range(g8, g8 + kcn)]
            emit_tgroup(srcs, AW8b[:, g8:g8 + kcn, R:R + E], "v", f"w8g{g8}",
                        pre_waits=("smalls", "consts"))
        emit_tgroup([LiMEsb[:, :]], LT_sb[:, :], "v", "limt",
                    pre_waits=("sconsts_v",))

        KB = D // BW

        def emit_mask_eighths(nt):
            KBQ4 = KQ // BW

            def emit_mload(nt, e):
                def _(g, nt=nt, e=e):
                    m = nt * 4 + e
                    if m >= 2:
                        pm = m - 2
                        wait(g, f"red1_{pm // 4}_{pm % 4}")
                    if nt > 0 and e < 2:
                        wait(g, f"bsum{nt-1}")
                    ks = e * KQ
                    qE[e % 2].inc(g.dma_start(out=maskb[:, e % 2, :],
                                  in_=mask_sl[nt * 128:nt * 128 + 128,
                                              ks:ks + KQ]), 16)
                    done(f"mload{nt}_{e}", qE[e % 2])
                blk.sync(_)

            def emit_mred(nt, e):
                def _(v, nt=nt, e=e):
                    wait(v, f"mload{nt}_{e}")
                    if nt > 0 and e == 0:
                        wait(v, f"bsum{nt-1}")
                    with nc.allow_low_precision(reason="mask block sums <=32 exact in bf16"):
                        cV.inc(v.reduce_sum(
                            red1w[:, e * KBQ4:(e + 1) * KBQ4],
                            maskb[:, e % 2, :].rearrange("p (g j) -> p g j", j=BW),
                            axis=AX.X))
                    vmark(v, f"red1_{nt}_{e}")
                blk.vector(_)

            emit_mload(nt, 0)
            emit_mload(nt, 1)
            for e in range(4):
                emit_mred(nt, e)
                if e + 2 < 4:
                    emit_mload(nt, e + 2)

        def emit_bm_chain(nt):
            def _(t_, nt=nt):
                for e in range(4):
                    wait(t_, f"red1_{nt}_{e}")
                if nt > 0:
                    wait(t_, f"bm01_{nt-1}")
                cT.inc(t_.matmul(pfA[0:4, 0:KB], E32b[:, :], red1w[:, 0:KB],
                                 start=True, stop=True))
                done(f"bsum{nt}", cT)
            blk.tensor(_)

            def _(v, nt=nt):
                wait(v, f"bsum{nt}")
                if nt > 0:
                    wait(v, f"bexp{nt-1}")
                cV.inc(v.tensor_single_scalar(bm4T[:, 0:KB], pfA[0:4, 0:KB],
                                              0.0, ALU.is_gt))
                vmark(v, f"bm01_{nt}")
            blk.vector(_)

            def _(t_, nt=nt):
                wait(t_, f"bm01_{nt}")
                if nt > 0:
                    wait(t_, f"bmex_{nt-1}")
                cT.inc(t_.matmul(pfB[:, 0:KB], Ematb[:, :], bm4T[:, 0:KB],
                                 start=True, stop=True))
                done(f"bexp{nt}", cT)
            blk.tensor(_)

            def _(v, nt=nt):
                wait(v, f"bexp{nt}")
                cV.inc(v.tensor_copy(bmex_all[:, nt, :], pfB[:, 0:KB]))
                vmark(v, f"bmex_{nt}")
            blk.vector(_)

        def emit_quarter(h):
            kh, nt = h // NTL, h % NTL
            rs, ks = nt * 128, kh * KQ
            kbq = KQ // BW

            def _(g, h=h, rs=rs, ks=ks):
                if h > 1:
                    wait(g, f"mtb{h-2}")
                qW[h % 2].inc(g.dma_start(out=wb[:, h % 2, :],
                              in_=w_sl[rs:rs + 128, ks:ks + KQ]), 16)
                qW[h % 2].inc(g.dma_start(out=Wbb[:, h % 2, :],
                              in_=Wf_sl[rs:rs + 128, ks:ks + KQ]), 16)
                done(f"wload{h}", qW[h % 2])
            blk.sync(_)

            def _(v, h=h, kh=kh, nt=nt, kbq=kbq):
                wait(v, f"wload{h}")
                wait(v, f"bmex_{nt}")
                if h > 0:
                    wait(v, f"mtb_done{h-1}")
                w3i = AP(wb, (h % 2) * KQ, [[2 * KQ, 128], [BW, kbq], [1, BW]])
                w3o = AP(Mtb, 0, [[KQ, 128], [BW, kbq], [1, BW]])
                b3 = AP(bmex_all, nt * KB + kh * kbq,
                        [[NTL * KB, 128], [1, kbq], [0, BW]])
                cV.inc(v.tensor_tensor(w3o, w3i, b3, ALU.mult))
                v.drain()
                cV.inc(v.tensor_tensor(Mtb[:, :], Mtb[:, :], Wbb[:, h % 2, :],
                                       ALU.add))
                vmark(v, f"mtb{h}")
            blk.vector(_)

            for g8 in range(0, KCQ, 8):
                kcn = min(8, KCQ - g8)
                kcg0 = kh * KCQ + g8
                srcs = [Mtb[:, kc2 * 128:(kc2 + 1) * 128]
                        for kc2 in range(g8, g8 + kcn)]
                emit_tgroup(srcs,
                            MTs[:, kcg0:kcg0 + kcn, nt * 128:(nt + 1) * 128],
                            "v", f"mtg{h}_{g8}", pre_waits=(f"mtb{h}",))
            ev[f"mtb_done{h}"] = ev[f"tp_mtg{h}_{(KCQ - 1) // 8 * 8}"]
            ev[f"q_done{h}"] = ev[f"cp_mtg{h}_{(KCQ - 1) // 8 * 8}"]

        qsched = {t: [] for t in range(NT)}
        cap = -(-NH // max(1, NT - 1))
        tile_of = {}
        prev_t = 1
        for h in range(NH):
            ntq = h % NTL
            t_assign = min(max(ntq, prev_t), NT - 1)
            while t_assign < NT - 1 and len(qsched[t_assign]) >= cap:
                t_assign += 1
            qsched[t_assign].append(h)
            tile_of[h] = t_assign
            prev_t = t_assign
        T_AG1 = max(tile_of[h] for h in range(2 * NTL))

        for t in range(NT):
            def _(sync, t=t):
                if t >= 2:
                    wait(sync, f"xnorm{t-2}")
                qX[t % 2].inc(sync.dma_start(out=xf[:, t % 2, :], in_=x3[:, t, :]), 16)
                done(f"x{t}", qX[t % 2])
            blk.sync(_)

            xtile = xf[:, t % 2, :]

            def _(s, t=t, xtile=xtile):
                wait(s, f"x{t}")
                if t == 0:
                    wait(s, "memsets")
                if t >= 2:
                    wait(s, f"zh{t-2}")
                cS.inc(s.activation(xnT[:, :, t * 128:(t + 1) * 128], xtile,
                                    AF.Copy, accum_out=st[:, 0:1]))
                s.drain()
                cS.inc(s.activation(xnT[:, :, t * 128:(t + 1) * 128], xtile,
                                    AF.Square, accum_out=st[:, 1:2]))
                s.drain()
                if t >= 1:
                    wait(s, f"xnorm{t-1}")
                cS.inc(s.mul(st[:, 2:3], st[:, 0:1], 1.0 / D))
                cS.inc(s.mul(st[:, 3:4], st[:, 1:2], 1.0 / D))
                s.drain()
                cS.inc(s.mul(st[:, 7:8], st[:, 2:3], st[:, 2:3]))
                s.drain()
                cS.inc(s.mul(st[:, 7:8], st[:, 7:8], -1.0))
                s.drain()
                cS.inc(s.add(st[:, 4:5], st[:, 3:4], st[:, 7:8]))
                s.drain()
                cS.inc(s.activation(st[:, 7:8], st[:, 4:5], AF.Sqrt,
                                    bias=epsc[:, 0:1]))
                smark(s, f"stats{t}")
            blk.scalar(_)

            def _(v, t=t, xtile=xtile):
                wait(v, f"stats{t}")
                if t >= 1:
                    wait(v, f"xn_done{t-1}")
                v.drain()
                cV.inc(v.reciprocal(st[:, 5:6], st[:, 7:8]))
                v.drain()
                cV.inc(v.tensor_scalar(st[:, 6:7], st[:, 2:3], st[:, 5:6], -1.0,
                                       ALU.mult, ALU.mult))
                v.drain()
                cV.inc(v.tensor_scalar(xnb[:, :], xtile, st[:, 5:6], st[:, 6:7],
                                       ALU.mult, ALU.add))
                vmark(v, f"xnorm{t}")
            blk.vector(_)

            NGRP = (KC + 7) // 8
            for gi in range(NGRP):
                kc0 = gi * 8
                kcn = min(8, KC - kc0)
                srcs = [xnb[:, kc * 128:(kc + 1) * 128]
                        for kc in range(kc0, kc0 + kcn)]
                emit_tgroup(srcs, xnT[:, kc0:kc0 + kcn, t * 128:(t + 1) * 128],
                            "v", f"xng{t}_{gi}",
                            pre_waits=(f"xnorm{t}",))
            ev[f"xn_done{t}"] = ev[f"tp_xng{t}_{NGRP-1}"]

            def _(t_, t=t, NGRP=NGRP):
                for gi in range(NGRP):
                    wait(t_, f"cp_xng{t}_{gi}")
                if t == 0:
                    wait(t_, "sconsts_v")
                    wait(t_, f"cp_w8g{(KC - 1) // 8 * 8}")
                    wait(t_, "memsets")
                if t >= 2:
                    wait(t_, f"zhfree{t % 2}")
                ztile = zh[t % 2]
                for kc in range(KC):
                    t_.matmul(ztile[:, 0:128], AW8b[:, kc, :],
                              xnT[:, kc, t * 128:(t + 1) * 128],
                              start=(kc == 0), stop=False)
                cT.inc(t_.matmul(ztile[:, 0:128], b8ext[:, :], ones128b[:, :],
                                 start=False, stop=True))
                done(f"zh{t}", cT)
            blk.tensor(_)

            def _(v, t=t):
                wait(v, f"zh{t}")
                cV.inc(v.tensor_copy(zT_sb[:, t * 128:(t + 1) * 128],
                                     zh[t % 2][0:R, 0:128]))
                cV.inc(v.tensor_copy(H8T_sb[:, t * 128:(t + 1) * 128],
                                     zh[t % 2][R:R + E, 0:128]))
                vmark(v, f"zcp{t}")
                done(f"hcp{t}", cV)
                done(f"zhfree{t % 2}", cV)
            blk.vector(_)

            if t < NTL:
                emit_mask_eighths(t)
                emit_bm_chain(t)
            for hq in qsched.get(t, []):
                emit_quarter(hq)

            if t == T_AG1:
                def _(g):
                    for h in range(2 * NTL):
                        wait(g, f"q_done{h}")
                    qD.inc(g.dma_start(out=ag_in1[:, :, :], in_=MTs[:, 0:KC2, :]), 16)
                    g.wait_ge(_qD, qD.n)
                    cc.inc(g.collective_compute(
                        "AllGather", ALU.bypass, replica_groups=RG,
                        ins=[ag_in1[:, :, :].opt()],
                        outs=[ag_out1[:, :, :, :].opt()]))
                    done("ag1", cc)
                blk.gpsimd(_)

        def _(v):
            for t in range(NT):
                wait(v, f"hcp{t}")
            cV.inc(v.reduce_max(mx2[:, 0:1], H8T_sb[:, 1:TPC:2], axis=AX.X,
                                apply_absolute_value=True))
            vmark(v, "hmax")
        blk.vector(_)

        def _(t_):
            for t in range(NT):
                wait(t_, f"zcp{t}")
            wait(t_, "cp_limt")
            wait(t_, "zhfree0"); wait(t_, "zhfree1")
            cT.inc(t_.matmul(zhA[0:E, 0:AN], LT_sb[:, :], zT_sb[:, 1:TPC:2],
                             start=True, stop=True))
            done("dTmm", cT)
        blk.tensor(_)

        def _(s):
            wait(s, "dTmm")
            cS.inc(s.copy(dT_sb[:, :], zhA[0:E, 0:AN]))
            smark(s, "dTcp")
        blk.scalar(_)

        def _(v):
            wait(v, "dTcp")
            cV.inc(v.reduce_max(mx2[:, 1:2], dT_sb[:, :], axis=AX.X,
                                apply_absolute_value=True))
            vmark(v, "dmax")
        blk.vector(_)

        def _(t_):
            wait(t_, "hmax"); wait(t_, "dmax")
            cT.inc(t_.transpose(zhB[0:2, 0:E], mx2[:, :], identf[0:E, 0:E]))
            done("mxT", cT)
        blk.tensor(_)

        def _(v):
            wait(v, "mxT")
            cV.inc(v.reduce_max(m21[:, :], zhB[0:2, 0:E], axis=AX.X))
            vmark(v, "m21")
        blk.vector(_)

        def _(t_):
            wait(t_, "m21")
            cT.inc(t_.transpose(zhA[0:1, 0:2], m21[:, :], identf[0:2, 0:2]))
            done("m21T", cT)
        blk.tensor(_)

        def _(v):
            wait(v, "m21T")
            cV.inc(v.memset(arst[:, :], 0.0))
            v.drain()
            cV.inc(v.tensor_copy(arst[0:1, 0:2], zhA[0:1, 0:2]))
            vmark(v, "arst")
        blk.vector(_)



        def _(g):
            for h in range(2 * NTL, NH):
                wait(g, f"q_done{h}")
            qD.inc(g.dma_start(out=ag_in2[:, :, :], in_=MTs[:, KC2:KC, :]), 16)
            g.wait_ge(_qD, qD.n)
            cc.inc(g.collective_compute(
                "AllGather", ALU.bypass, replica_groups=RG,
                ins=[ag_in2[:, :, :].opt()],
                outs=[ag_out2[:, :, :, :].opt()]))
            done("ag2", cc)
        blk.gpsimd(_)

        def _(g):
            wait(g, "arst")
            qC.inc(g.dma_start(out=ar_in[:, :], in_=arst[:, :]), 16)
            g.wait_ge(_qC, qC.n)
            cc.inc(g.collective_compute("AllReduce", ALU.max, replica_groups=RG,
                                        ins=[ar_in[:, :].opt()],
                                        outs=[ar_out[:, :].opt()]))
            done("ar", cc)
            g.wait_ge(_cc, cc.n)
            qC.inc(g.dma_start(out=row12[:, :], in_=ar_out[:, 0:2]), 16)
            done("scales_in", qC)
        blk.gpsimd(_)

        def _(v):
            wait(v, "scales_in")
            cV.inc(v.tensor_scalar_max(row12[:, :], row12[:, :], EPS_G))
            v.drain()
            cV.inc(v.reciprocal(row12[:, :], row12[:, :]))
            v.drain()
            cV.inc(v.tensor_scalar(row12[:, :], row12[:, :], GAMMA, None, ALU.mult))
            vmark(v, "scales")
        blk.vector(_)

        def _(t_):
            wait(t_, "scales")
            cT.inc(t_.matmul(zhB[0:E, 0:2], ones8f[:, :], row12[:, :],
                             start=True, stop=True))
            done("rHdmm", cT)
        blk.tensor(_)

        def _(v):
            wait(v, "rHdmm")
            cV.inc(v.tensor_copy(rHd[:, :], zhB[0:E, 0:2]))
            v.drain()
            cV.inc(v.tensor_scalar(ltb[:, :], dT_sb[:, :], rHd[:, 1:2],
                                   None, ALU.mult))
            v.drain()
            cV.inc(v.scalar_tensor_tensor(ltb[:, :], H8T_sb[:, 1:TPC:2],
                                          rHd[:, 0:1], ltb[:, :],
                                          ALU.mult, ALU.add))
            vmark(v, "ltb")
            done("rHd_done", cV)
        blk.vector(_)

        for a in range(NA):
            emit_tgroup([ltb[:, a * 128:(a + 1) * 128]],
                        garb[:, a * 8:(a + 1) * 8], "v",
                        f"gl{a}", pre_waits=("ltb",))

        def _(v):
            for a in range(NA):
                wait(v, f"cp_gl{a}")
            for a in range(NA):
                gl = garb[:, a * 8:(a + 1) * 8]
                cV.inc(v.reduce_max(garb[:, 64 + a:65 + a], gl, axis=AX.X,
                                    negate=True))
            vmark(v, "negm")
        blk.vector(_)

        def _(s):
            wait(s, "negm")
            for a in range(NA):
                cS.inc(s.activation(garb[:, 32 + a * 8:40 + a * 8],
                                    garb[:, a * 8:(a + 1) * 8], AF.Exp,
                                    bias=garb[:, 64 + a:65 + a]))
            smark(s, "gexp")
        blk.scalar(_)

        def _(v):
            wait(v, "gexp")
            m1pad = garb[:, 72:80]
            cV.inc(v.memset(garb[:, 73:80], -5.0))
            v.drain()
            for a in range(NA):
                ge = garb[:, 32 + a * 8:40 + a * 8]
                gp = garb[:, a * 8:(a + 1) * 8]
                ssum = garb[:, 68 + a:69 + a]
                th = garb[:, 80 + a:81 + a]
                cV.inc(v.reduce_sum(ssum, ge, axis=AX.X))
                v.drain()
                cV.inc(v.reciprocal(ssum, ssum))
                v.drain()
                cV.inc(v.tensor_scalar(gp, ge, ssum, None, ALU.mult))
                v.drain()
                cV.inc(v.reduce_max(garb[:, 72:73], gp, axis=AX.X))
                v.drain()
                cV.inc(v.match_replace(ge, m1pad, gp, -1e30))
                v.drain()
                cV.inc(v.reduce_max(th, ge, axis=AX.X))
                v.drain()
                cV.inc(v.tensor_scalar(th, th, -2.0, None, ALU.mult))
                v.drain()
            vmark(v, "gprobs")
        blk.vector(_)

        def _(s):
            wait(s, "gprobs")
            for a in range(NA):
                cS.inc(s.activation(garb[:, 32 + a * 8:40 + a * 8],
                                    garb[:, a * 8:(a + 1) * 8], AF.Sigmoid,
                                    bias=garb[:, 80 + a:81 + a], scale=2.0))
            smark(s, "gsig")
        blk.scalar(_)

        def _(v):
            wait(v, "gsig")
            for a in range(NA):
                gp = garb[:, a * 8:(a + 1) * 8]
                gs = garb[:, 32 + a * 8:40 + a * 8]
                s2 = garb[:, 68 + a:69 + a]
                cV.inc(v.tensor_tensor(gp, gp, gs, ALU.mult))
                v.drain()
                cV.inc(v.reduce_sum(s2, gp, axis=AX.X))
                v.drain()
                cV.inc(v.tensor_scalar(s2, s2, 1e-9, None, ALU.add))
                v.drain()
                cV.inc(v.reciprocal(s2, s2))
                v.drain()
                cV.inc(v.tensor_scalar(gbf[:, a * 8:(a + 1) * 8], gp, s2,
                                       None, ALU.mult))
                v.drain()
            vmark(v, "gwn")
        blk.vector(_)

        for a in range(NA):
            emit_tgroup([gbf[:, a * 8:(a + 1) * 8]], wnT_sb[:, :], "v",
                        f"wnT{a}", pre_waits=("gwn",))

            def _(v, a=a):
                wait(v, f"cp_wnT{a}")
                cV.inc(v.tensor_copy(AP(wfT, a * 256, [[TPC, E], [2, 128]]),
                                     wnT_sb[:, :]))
                cV.inc(v.tensor_copy(AP(wfT, a * 256 + 1, [[TPC, E], [2, 128]]),
                                     wnT_sb[:, :]))
                vmark(v, f"wfT{a}")
            blk.vector(_)

        def _(t_):
            for a in range(NA):
                wait(t_, f"wfT{a}")
            wait(t_, "rHd_done")
            for i, off in enumerate(range(0, TPC, 512)):
                wdt = min(512, TPC - off)
                cT.inc(t_.matmul(zh[i % 2][0:R, 0:wdt], LiMEsb[:, :],
                                 wfT[:, off:off + wdt], start=True, stop=True))
            done("pmix", cT)
        blk.tensor(_)

        def _(v):
            wait(v, "pmix")
            for i, off in enumerate(range(0, TPC, 512)):
                wdt = min(512, TPC - off)
                cV.inc(v.tensor_copy(pm_sb[:, off:off + wdt],
                                     zh[i % 2][0:R, 0:wdt]))
            v.drain()
            cV.inc(v.tensor_tensor(zpT[0:R, :], zT_sb[:, :], pm_sb[:, :], ALU.mult))
            vmark(v, "zpT")
        blk.vector(_)



        MTo_v = (xf[:, :, :].rearrange("p a d -> p (a d)").bitcast(BF16)
                 .rearrange("p (c n) -> p c n", n=NSL)[:, 0:KC, :])
        mtbuf = [MTs[:, :, :], MTo_v]
        for r in range(NCORES):
            def _(sync, r=r):
                wait(sync, "ag1")
                if r == 1:
                    wait(sync, f"xnorm{NT-1}")
                if r >= 2:
                    wait(sync, f"mm_r{r-2}_t{NT-1}")
                qM[r % 2].inc(sync.dma_start(out=mtbuf[r % 2][:, 0:KC2, :],
                                             in_=ag_out1[r, :, :, :]), 16)
                done(f"mtload{r}h1", qM[r % 2])
                wait(sync, "ag2")
                qM2[r % 2].inc(sync.dma_start(out=mtbuf[r % 2][:, KC2:KC, :],
                                              in_=ag_out2[r, :, :, :]), 16)
                done(f"mtload{r}h2", qM2[r % 2])
            blk.sync(_)

            for t in range(NT):
                it = r * NT + t
                DEF = 2
                NIT = NCORES * NT

                def _(t_, r=r, t=t, it=it, DEF=DEF, NIT=NIT):
                    wait(t_, f"mtload{r}h1")
                    if it >= 4:
                        itp = it - 4
                        wait(t_, f"ev_r{itp // NT}_t{itp % NT}")
                    if r == 0 and t == 0:
                        wait(t_, "W8b_Bext"); wait(t_, "Bext_row")
                    if it >= DEF:
                        itd = it - DEF
                        rd_, td_ = itd // NT, itd % NT
                        if itd == 0:
                            wait(t_, "zpT")
                        cT.inc(t_.matmul(pp[itd % 4][:, 0:NSL],
                                         zpT[:, td_ * 128:(td_ + 1) * 128],
                                         Bextb[:, rd_ * NSL:(rd_ + 1) * NSL],
                                         start=False, stop=True))
                        done(f"mm_r{rd_}_t{td_}", cT)
                    ppt = pp[it % 4]
                    for kc in range(KC2):
                        t_.matmul(ppt[:, 0:NSL], xnT[:, kc, t * 128:(t + 1) * 128],
                                  mtbuf[r % 2][:, kc, :], start=(kc == 0), stop=False)
                    wait(t_, f"mtload{r}h2")
                    for kc in range(KC2, KC):
                        ins = t_.matmul(ppt[:, 0:NSL], xnT[:, kc, t * 128:(t + 1) * 128],
                                        mtbuf[r % 2][:, kc, :], start=False, stop=False)
                    cT.inc(ins)
                    done(f"mmp_r{r}_t{t}", cT)
                    if it == NIT - 1:
                        for itd in range(NIT - DEF, NIT):
                            rd_, td_ = itd // NT, itd % NT
                            if itd == 0:
                                wait(t_, "zpT")
                            cT.inc(t_.matmul(pp[itd % 4][:, 0:NSL],
                                             zpT[:, td_ * 128:(td_ + 1) * 128],
                                             Bextb[:, rd_ * NSL:(rd_ + 1) * NSL],
                                             start=False, stop=True))
                            done(f"mm_r{rd_}_t{td_}", cT)
                blk.tensor(_)

                ready = [it - DEF] if it >= DEF else []
                for itd in ready:
                    rd_, td_ = itd // NT, itd % NT

                    def _(v, rd_=rd_, td_=td_, itd=itd):
                        wait(v, f"mm_r{rd_}_t{td_}")
                        wait(v, f"xr_{rd_}_{td_}")
                        if itd >= 3:
                            itp = itd - 3
                            wait(v, f"st_r{itp // NT}_t{itp % NT}")
                        cV.inc(v.tensor_tensor(stg[:, itd % 3, :],
                                               pp[itd % 4][:, 0:NSL],
                                               xr[:, itd % 2, :], ALU.add))
                        vmark(v, f"ev_r{rd_}_t{td_}")
                    blk.vector(_)

                    def _(g, rd_=rd_, td_=td_, itd=itd):
                        wait(g, f"ev_r{rd_}_t{td_}")
                        sx = sS[itd % 3]
                        sx.inc(g.dma_start(
                            out=out3[:, td_, rd_ * NSL:(rd_ + 1) * NSL],
                            in_=stg[:, itd % 3, :]), 16)
                        done(f"st_r{rd_}_t{td_}", sx)
                    blk.gpsimd(_)

                def _(sync, r=r, t=t, it=it):
                    if it >= 2:
                        itp = it - 2
                        wait(sync, f"ev_r{itp // NT}_t{itp % NT}")
                    qR[it % 2].inc(sync.dma_start(out=xr[:, it % 2, :],
                                   in_=x3[:, t, r * NSL:(r + 1) * NSL]), 16)
                    done(f"xr_{r}_{t}", qR[it % 2])
                blk.sync(_)

        NIT = NCORES * NT
        for itd in range(NIT - 2, NIT):
            rd_, td_ = itd // NT, itd % NT

            def _(v, rd_=rd_, td_=td_, itd=itd):
                wait(v, f"mm_r{rd_}_t{td_}")
                wait(v, f"xr_{rd_}_{td_}")
                if itd >= 3:
                    itp = itd - 3
                    wait(v, f"st_r{itp // NT}_t{itp % NT}")
                cV.inc(v.tensor_tensor(stg[:, itd % 3, :],
                                       pp[itd % 4][:, 0:NSL],
                                       xr[:, itd % 2, :], ALU.add))
                vmark(v, f"ev_r{rd_}_t{td_}")
            blk.vector(_)

            def _(g, rd_=rd_, td_=td_, itd=itd):
                wait(g, f"ev_r{rd_}_t{td_}")
                sx = sS[itd % 3]
                sx.inc(g.dma_start(
                    out=out3[:, td_, rd_ * NSL:(rd_ + 1) * NSL],
                    in_=stg[:, itd % 3, :]), 16)
                done(f"st_r{rd_}_t{td_}", sx)
            blk.gpsimd(_)

        def _(g):
            for scnt in sS:
                g.wait_ge(scnt.sem, scnt.n)
        blk.gpsimd(_)

    return nc


def shard_inputs(data, mask, weight, bias, W, b, A, Bmat, LiMEs, TPC, D):
    xflat = np.ascontiguousarray(np.asarray(data, np.float32).reshape(-1, D))
    mask = np.asarray(mask, np.int32)
    weight = np.asarray(weight, np.float32)
    W = np.asarray(W, np.float32)
    NSL = D // NCORES
    identb = np.eye(128, dtype=ml_dtypes.bfloat16)
    identf = np.eye(128, dtype=np.float32)
    E32 = np.zeros((128, 4), dtype=ml_dtypes.bfloat16)
    for n in range(128):
        E32[n, n // 32] = 1
    Emat = np.ascontiguousarray(E32.T)
    in_maps = []
    for c in range(NCORES):
        in_maps.append({
            "x": np.ascontiguousarray(xflat[c * TPC:(c + 1) * TPC]),
            "mask_sl": np.ascontiguousarray(mask[c * NSL:(c + 1) * NSL]),
            "w_sl": np.ascontiguousarray(weight[c * NSL:(c + 1) * NSL]),
            "Wf_sl": np.ascontiguousarray(W[c * NSL:(c + 1) * NSL]),
            "W8": np.ascontiguousarray(W[0:E]),
            "biasrow": np.asarray(bias, np.float32).reshape(1, D),
            "brow": np.asarray(b, np.float32).reshape(1, D),
            "A": np.asarray(A, np.float32),
            "Bmat": np.asarray(Bmat, np.float32),
            "LiMEs": np.asarray(LiMEs, np.float32),
            "identb": identb,
            "identf": identf,
            "E32": E32,
            "Emat": Emat,
        })
    return in_maps



from concourse.bass_utils import run_bass_kernel_spmd

TPC_REAL = 1024
D_REAL = 4096


def kernel(L0_attn_data, L0_attn_mask, L0_attn_weight, L0_attn_bias,
           L0_ffn_W, L0_ffn_b, L0_ffn_A, L0_ffn_Bmat, L0_ffn_LiMEs):
    TPC, D = TPC_REAL, D_REAL
    nc = build_program(TPC, D, detect_races=True)
    in_maps = shard_inputs(L0_attn_data, L0_attn_mask, L0_attn_weight,
                           L0_attn_bias, L0_ffn_W, L0_ffn_b, L0_ffn_A,
                           L0_ffn_Bmat, L0_ffn_LiMEs, TPC, D)
    res = run_bass_kernel_spmd(nc, in_maps, core_ids=list(range(NCORES)))
    out = np.concatenate([np.asarray(r["out"]) for r in res.results], axis=0)
    return out.reshape(4, 2048, D).astype(np.float32, copy=False)


# version 13
# speedup vs baseline: 1.0446x; 1.0446x over previous
import numpy as np
import ml_dtypes
import concourse.bass as bass
import concourse.mybir as mybir
from concourse.bass_types import AP

F32 = mybir.dt.float32
BF16 = mybir.dt.bfloat16
I32 = mybir.dt.int32
AF = mybir.ActivationFunctionType
ALU = mybir.AluOpType
AX = mybir.AxisListType

NCORES = 8
R = 32
E = 8
BW = 32
EPS_LN = 1e-5
EPS_G = 1e-6
GAMMA = 0.5
SCALING = 0.5


class Ctr:
    def __init__(self, sem):
        self.sem = sem
        self.n = 0

    def inc(self, instr, k=1):
        instr.then_inc(self.sem, k)
        self.n += k
        return self.n


def build_program(TPC, D, detect_races=True):
    NT = TPC // 128
    KC = D // 128
    KC2 = KC // 2
    NSL = D // NCORES
    KQ = D // 4
    KCQ = KC // 4
    NTL = NSL // 128
    KE = KQ // 2
    KBE = KE // BW
    AN = TPC // 2
    NA = AN // 128
    NH = 4 * NTL
    assert TPC % 256 == 0 and D % 256 == 0 and NSL % 128 == 0

    nc = bass.Bass("TRN2", target_bir_lowering=False,
                   detect_race_conditions=detect_races)

    x_in = nc.dram_tensor("x", [TPC, D], F32, kind="ExternalInput")
    mask_sl = nc.dram_tensor("mask_sl", [NSL, D], I32, kind="ExternalInput")
    w_sl = nc.dram_tensor("w_sl", [NSL, D], F32, kind="ExternalInput")
    Wf_sl = nc.dram_tensor("Wf_sl", [NSL, D], F32, kind="ExternalInput")
    W8_in = nc.dram_tensor("W8", [E, D], F32, kind="ExternalInput")
    biasrow_in = nc.dram_tensor("biasrow", [1, D], F32, kind="ExternalInput")
    brow_in = nc.dram_tensor("brow", [1, D], F32, kind="ExternalInput")
    A_in = nc.dram_tensor("A", [D, R], F32, kind="ExternalInput")
    Bmat_in = nc.dram_tensor("Bmat", [R, D], F32, kind="ExternalInput")
    LiMEs_in = nc.dram_tensor("LiMEs", [E, R], F32, kind="ExternalInput")
    identb_in = nc.dram_tensor("identb", [128, 128], BF16, kind="ExternalInput")
    identf_in = nc.dram_tensor("identf", [128, 128], F32, kind="ExternalInput")
    E32_in = nc.dram_tensor("E32", [128, 4], BF16, kind="ExternalInput")
    Emat_in = nc.dram_tensor("Emat", [4, 128], BF16, kind="ExternalInput")
    out_d = nc.dram_tensor("out", [TPC, D], F32, kind="ExternalOutput")

    ag_in1 = nc.dram_tensor("ag_in1", [128, KC2, NSL], BF16)
    ag_in2 = nc.dram_tensor("ag_in2", [128, KC2, NSL], BF16)
    ag_out1 = nc.dram_tensor("ag_out1", [NCORES, 128, KC2, NSL], BF16,
                             addr_space="Shared")
    ag_out2 = nc.dram_tensor("ag_out2", [NCORES, 128, KC2, NSL], BF16,
                             addr_space="Shared")
    ar_in = nc.dram_tensor("ar_in", [1, 8], F32)
    ar_out = nc.dram_tensor("ar_out", [1, 8], F32)
    RG = [list(range(NCORES))]

    x3 = x_in[:, :].rearrange("(t p) d -> p t d", p=128)
    out3 = out_d[:, :].rearrange("(t p) d -> p t d", p=128)

    from contextlib import ExitStack
    ES = ExitStack()
    if True:
        xnT = ES.enter_context(nc.sbuf_tensor("xnT", [128, KC, TPC], BF16))
        AW8b = ES.enter_context(nc.sbuf_tensor("AW8b", [128, KC, R + E], BF16))
        zT_sb = ES.enter_context(nc.sbuf_tensor("zT_sb", [R, TPC], BF16))
        H8T_sb = ES.enter_context(nc.sbuf_tensor("H8T_sb", [E, TPC], BF16))
        dT_sb = ES.enter_context(nc.sbuf_tensor("dT_sb", [E, AN], BF16))
        ltb = ES.enter_context(nc.sbuf_tensor("ltb", [E, AN], BF16))
        wfT = ES.enter_context(nc.sbuf_tensor("wfT", [E, TPC], BF16))
        pm_sb = ES.enter_context(nc.sbuf_tensor("pm_sb", [R, TPC], BF16))
        zpT = ES.enter_context(nc.sbuf_tensor("zpT", [R + 1, TPC], BF16))
        Bextb = ES.enter_context(nc.sbuf_tensor("Bextb", [R + 1, D], BF16))
        b8ext = ES.enter_context(nc.sbuf_tensor("b8ext", [1, R + E], BF16))
        ones128b = ES.enter_context(nc.sbuf_tensor("ones128b", [1, 128], BF16))
        ones8f = ES.enter_context(nc.sbuf_tensor("ones8f", [1, E], F32))
        LiMEsb = ES.enter_context(nc.sbuf_tensor("LiMEsb", [E, R], BF16))
        LT_sb = ES.enter_context(nc.sbuf_tensor("LT_sb", [R, E], BF16))
        identb = ES.enter_context(nc.sbuf_tensor("identb_s", [128, 128], BF16))
        identf = ES.enter_context(nc.sbuf_tensor("identf_s", [128, 128], F32))
        E32b = ES.enter_context(nc.sbuf_tensor("E32b_s", [128, 4], BF16))
        Ematb = ES.enter_context(nc.sbuf_tensor("Ematb", [4, 128], BF16))
        st = ES.enter_context(nc.sbuf_tensor("stats", [128, 8], F32))
        epsc = ES.enter_context(nc.sbuf_tensor("epsc", [128, 1], F32))
        mx2 = ES.enter_context(nc.sbuf_tensor("mx2", [E, 2], F32))
        m21 = ES.enter_context(nc.sbuf_tensor("m21", [2, 1], F32))
        row12 = ES.enter_context(nc.sbuf_tensor("row12", [1, 2], F32))
        arst = ES.enter_context(nc.sbuf_tensor("arst", [1, 8], F32))
        rHd = ES.enter_context(nc.sbuf_tensor("rHd", [E, 2], F32))
        garb = ES.enter_context(nc.sbuf_tensor("garb", [128, 84], F32))
        gbf = ES.enter_context(nc.sbuf_tensor("gbf", [128, 32], BF16))
        wnT_sb = ES.enter_context(nc.sbuf_tensor("wnT_sb", [E, 128], BF16))
        W8b = ES.enter_context(nc.sbuf_tensor("W8b", [E, D], BF16))
        xf = ES.enter_context(nc.sbuf_tensor("xf", [128, 2, D], F32))
        xnb = ES.enter_context(nc.sbuf_tensor("xnb", [128, D], BF16))
        maskb = ES.enter_context(nc.sbuf_tensor("maskb", [128, 2, KQ], I32))
        wb = ES.enter_context(nc.sbuf_tensor("wb", [128, 2, KQ], F32))
        Wbb = ES.enter_context(nc.sbuf_tensor("Wbb", [128, 2, KQ], F32))
        red1w = ES.enter_context(nc.sbuf_tensor("red1w", [128, D // BW // 4 * 4], BF16))
        bm4T = ES.enter_context(nc.sbuf_tensor("bm4T", [4, D // BW], BF16))
        bmex_all = ES.enter_context(nc.sbuf_tensor("bmex_all", [128, NTL, D // BW], F32))
        Mtb = ES.enter_context(nc.sbuf_tensor("Mtb", [128, KQ], BF16))
        MTs = ES.enter_context(nc.sbuf_tensor("MTs", [128, KC, NSL], BF16))
        xr = ES.enter_context(nc.sbuf_tensor("xr", [128, 2, NSL], F32))
        stg = ES.enter_context(nc.sbuf_tensor("stg", [128, 3, NSL], F32))
        ppA = ES.enter_context(nc.psum_tensor("ppA", [128, 512], F32))
        ppB = ES.enter_context(nc.psum_tensor("ppB", [128, 512], F32))
        zhA = ES.enter_context(nc.psum_tensor("zhA", [R + E, 512], F32))
        zhB = ES.enter_context(nc.psum_tensor("zhB", [R + E, 512], F32))
        ptA = ES.enter_context(nc.psum_tensor("ptA", [128, 1024], BF16))
        ptB = ES.enter_context(nc.psum_tensor("ptB", [128, 1024], BF16))
        pfA = ES.enter_context(nc.psum_tensor("pfA", [128, 512], F32))
        pfB = ES.enter_context(nc.psum_tensor("pfB", [128, 512], F32))
        _qA = ES.enter_context(nc.semaphore("qA"))
        _qG = ES.enter_context(nc.semaphore("qG"))
        _qB = ES.enter_context(nc.semaphore("qB"))
        _qC = ES.enter_context(nc.semaphore("qC"))
        _qD = ES.enter_context(nc.semaphore("qD"))
        _cT = ES.enter_context(nc.semaphore("cT"))
        _cV = ES.enter_context(nc.semaphore("cV"))
        _cS = ES.enter_context(nc.semaphore("cS"))
        _cG = ES.enter_context(nc.semaphore("cG"))
        _cc = ES.enter_context(nc.semaphore("cc"))
        _qX0 = ES.enter_context(nc.semaphore("qX0"))
        _qX1 = ES.enter_context(nc.semaphore("qX1"))
        _qM0 = ES.enter_context(nc.semaphore("qM0"))
        _qM1 = ES.enter_context(nc.semaphore("qM1"))
        _qM2 = ES.enter_context(nc.semaphore("qM2"))
        _qM3 = ES.enter_context(nc.semaphore("qM3"))
        _qR0 = ES.enter_context(nc.semaphore("qR0"))
        _qR1 = ES.enter_context(nc.semaphore("qR1"))
        _qW0 = ES.enter_context(nc.semaphore("qW0"))
        _qW1 = ES.enter_context(nc.semaphore("qW1"))
        _qE0 = ES.enter_context(nc.semaphore("qE0"))
        _qE1 = ES.enter_context(nc.semaphore("qE1"))
        _qF = ES.enter_context(nc.semaphore("qF"))
        _s0 = ES.enter_context(nc.semaphore("s0"))
        _s1 = ES.enter_context(nc.semaphore("s1"))
        _s2 = ES.enter_context(nc.semaphore("s2"))
        blk = ES.enter_context(nc.Block())
    with ES:
        qA, qG, qB = Ctr(_qA), Ctr(_qG), Ctr(_qB)
        qC, qD = Ctr(_qC), Ctr(_qD)
        cT, cV, cS, cG, cc = Ctr(_cT), Ctr(_cV), Ctr(_cS), Ctr(_cG), Ctr(_cc)
        qX = [Ctr(_qX0), Ctr(_qX1)]
        qM = [Ctr(_qM0), Ctr(_qM1)]
        qM2 = [Ctr(_qM2), Ctr(_qM3)]
        qR = [Ctr(_qR0), Ctr(_qR1)]
        qW = [Ctr(_qW0), Ctr(_qW1)]
        qE = [Ctr(_qE0), Ctr(_qE1)]
        qF = Ctr(_qF)
        sS = [Ctr(_s0), Ctr(_s1), Ctr(_s2)]
        pp = [ppA, ppB, pfA, pfB]
        zh = [zhA, zhB]
        pt = [ptA, ptB]
        ev = {}

        def done(name, ctr):
            ev[name] = (ctr, ctr.n)

        def wait(eng, name):
            c, v = ev[name]
            if v > 0:
                eng.wait_ge(c.sem, v)

        def vmark(v, name):
            cV.inc(v.drain())
            done(name, cV)

        def smark(s, name):
            cS.inc(s.drain())
            done(name, cS)

        tpar = [0]

        def emit_tgroup(srcs, dst_ap, copy_engine, name, pre_waits=()):
            par = tpar[0] % 2
            tpar[0] += 1
            prev = ev.get(f"ptfree{par}")
            w_out = srcs[0].shape[1]

            def _t(t):
                for wn_ in pre_waits:
                    wait(t, wn_)
                if prev is not None and prev[1] > 0:
                    t.wait_ge(prev[0].sem, prev[1])
                off = 0
                for i, sap in enumerate(srcs):
                    p_in = sap.shape[0]
                    ins = t.transpose(pt[par][0:w_out, off:off + p_in], sap,
                                      identb[0:p_in, 0:p_in])
                    if i == len(srcs) - 1:
                        cT.inc(ins)
                    off += p_in
                done(f"tp_{name}", cT)
            blk.tensor(_t)

            def _c(c):
                wait(c, f"tp_{name}")
                total = sum(sap.shape[0] for sap in srcs)
                srcp = pt[par][0:w_out, 0:total]
                if dst_ap.ndim == 3:
                    srcp = srcp.rearrange("p (a b) -> p a b", b=dst_ap.shape[2])
                if copy_engine == "v":
                    cV.inc(c.tensor_copy(dst_ap, srcp))
                    done(f"ptfree{par}", cV)
                    done(f"cp_{name}", cV)
                else:
                    cS.inc(c.copy(dst_ap, srcp))
                    done(f"ptfree{par}", cS)
                    done(f"cp_{name}", cS)
            (blk.vector if copy_engine == "v" else blk.scalar)(_c)

        def _(g):
            qG.inc(g.dma_start(out=AW8b[:, :, 0:R],
                               in_=A_in[:, :].rearrange("(c k) r -> k c r", k=128)), 16)
            qG.inc(g.dma_start(out=W8b[:, :], in_=W8_in[:, :]), 16)
            qG.inc(g.dma_start(out=LiMEsb[:, :], in_=LiMEs_in[:, :]), 16)

            qG.inc(g.dma_start(out=Bextb[0:R, :], in_=Bmat_in[:, :]), 16)
            qG.inc(g.dma_start(out=identb[:, :], in_=identb_in[:, :]), 16)
            qG.inc(g.dma_start(out=identf[:, :], in_=identf_in[:, :]), 16)
            qG.inc(g.dma_start(out=E32b[:, :], in_=E32_in[:, :]), 16)
            qG.inc(g.dma_start(out=Ematb[:, :], in_=Emat_in[:, :]), 16)
            for nm in ("smalls", "consts"):
                done(nm, qG)
            qB.inc(g.dma_start(out=Bextb[R:R + 1, :], in_=brow_in[:, :]), 16)
            g.wait_ge(_qB, 16)
            qB.inc(g.dma_start(out=Bextb[R:R + 1, :], in_=biasrow_in[:, :],
                               accum_op=ALU.add), 16)
            cG.inc(g.memset(ones128b[:, :], 1.0))
            cG.inc(g.memset(ones8f[:, :], 1.0))
            cG.inc(g.memset(b8ext[:, :], 0.0))
            cG.inc(g.memset(zpT[R:R + 1, :], 1.0))
            cG.inc(g.memset(epsc[:, :], EPS_LN))
            done("memsets", cG)
            g.wait_ge(_cG, cG.n)
            qF.inc(g.dma_start(out=b8ext[0:1, R:R + E], in_=brow_in[0:1, 0:E],
                               allow_slow_non_contiguous=True), 16)
            done("b8l", qF)
        blk.gpsimd(_)
        ev["Bext_row"] = (qB, 32)

        ev["sconsts_v"] = ev["b8l"]

        def _(s):
            wait(s, "smalls")
            cS.inc(s.mul(Bextb[0:R, :], Bextb[0:R, :], SCALING))
            smark(s, "W8b_Bext")
        blk.scalar(_)

        for g8 in range(0, KC, 8):
            kcn = min(8, KC - g8)
            srcs = [W8b[:, kc * 128:(kc + 1) * 128] for kc in range(g8, g8 + kcn)]
            emit_tgroup(srcs, AW8b[:, g8:g8 + kcn, R:R + E], "v", f"w8g{g8}",
                        pre_waits=("smalls", "consts"))
        emit_tgroup([LiMEsb[:, :]], LT_sb[:, :], "v", "limt",
                    pre_waits=("sconsts_v",))

        KB = D // BW

        def emit_mask_eighths(nt):
            KBQ4 = KQ // BW

            def emit_mload(nt, e):
                def _(g, nt=nt, e=e):
                    m = nt * 4 + e
                    if m >= 2:
                        pm = m - 2
                        wait(g, f"red1_{pm // 4}_{pm % 4}")
                    if nt > 0 and e < 2:
                        wait(g, f"bsum{nt-1}")
                    ks = e * KQ
                    qE[e % 2].inc(g.dma_start(out=maskb[:, e % 2, :],
                                  in_=mask_sl[nt * 128:nt * 128 + 128,
                                              ks:ks + KQ]), 16)
                    done(f"mload{nt}_{e}", qE[e % 2])
                blk.gpsimd(_)

            def emit_mred(nt, e):
                def _(v, nt=nt, e=e):
                    wait(v, f"mload{nt}_{e}")
                    if nt > 0 and e == 0:
                        wait(v, f"bsum{nt-1}")
                    with nc.allow_low_precision(reason="mask block sums <=32 exact in bf16"):
                        cV.inc(v.reduce_sum(
                            red1w[:, e * KBQ4:(e + 1) * KBQ4],
                            maskb[:, e % 2, :].rearrange("p (g j) -> p g j", j=BW),
                            axis=AX.X))
                    vmark(v, f"red1_{nt}_{e}")
                blk.vector(_)

            emit_mload(nt, 0)
            emit_mload(nt, 1)
            for e in range(4):
                emit_mred(nt, e)
                if e + 2 < 4:
                    emit_mload(nt, e + 2)

        def emit_bm_chain(nt):
            def _(t_, nt=nt):
                for e in range(4):
                    wait(t_, f"red1_{nt}_{e}")
                if nt > 0:
                    wait(t_, f"bm01_{nt-1}")
                cT.inc(t_.matmul(pfA[0:4, 0:KB], E32b[:, :], red1w[:, 0:KB],
                                 start=True, stop=True))
                done(f"bsum{nt}", cT)
            blk.tensor(_)

            def _(v, nt=nt):
                wait(v, f"bsum{nt}")
                if nt > 0:
                    wait(v, f"bexp{nt-1}")
                cV.inc(v.tensor_single_scalar(bm4T[:, 0:KB], pfA[0:4, 0:KB],
                                              0.0, ALU.is_gt))
                vmark(v, f"bm01_{nt}")
            blk.vector(_)

            def _(t_, nt=nt):
                wait(t_, f"bm01_{nt}")
                if nt > 0:
                    wait(t_, f"bmex_{nt-1}")
                cT.inc(t_.matmul(pfB[:, 0:KB], Ematb[:, :], bm4T[:, 0:KB],
                                 start=True, stop=True))
                done(f"bexp{nt}", cT)
            blk.tensor(_)

            def _(v, nt=nt):
                wait(v, f"bexp{nt}")
                cV.inc(v.tensor_copy(bmex_all[:, nt, :], pfB[:, 0:KB]))
                vmark(v, f"bmex_{nt}")
            blk.vector(_)

        def emit_quarter(h):
            kh, nt = h // NTL, h % NTL
            rs, ks = nt * 128, kh * KQ
            kbq = KQ // BW

            def _(g, h=h, rs=rs, ks=ks):
                if h > 1:
                    wait(g, f"mtb{h-2}")
                qW[h % 2].inc(g.dma_start(out=wb[:, h % 2, :],
                              in_=w_sl[rs:rs + 128, ks:ks + KQ]), 16)
                qW[h % 2].inc(g.dma_start(out=Wbb[:, h % 2, :],
                              in_=Wf_sl[rs:rs + 128, ks:ks + KQ]), 16)
                done(f"wload{h}", qW[h % 2])
            blk.gpsimd(_)

            def _(v, h=h, kh=kh, nt=nt, kbq=kbq):
                wait(v, f"wload{h}")
                wait(v, f"bmex_{nt}")
                if h > 0:
                    wait(v, f"mtb_done{h-1}")
                w3i = AP(wb, (h % 2) * KQ, [[2 * KQ, 128], [BW, kbq], [1, BW]])
                w3o = AP(Mtb, 0, [[KQ, 128], [BW, kbq], [1, BW]])
                b3 = AP(bmex_all, nt * KB + kh * kbq,
                        [[NTL * KB, 128], [1, kbq], [0, BW]])
                cV.inc(v.tensor_tensor(w3o, w3i, b3, ALU.mult))
                v.drain()
                cV.inc(v.tensor_tensor(Mtb[:, :], Mtb[:, :], Wbb[:, h % 2, :],
                                       ALU.add))
                vmark(v, f"mtb{h}")
            blk.vector(_)

            for g8 in range(0, KCQ, 8):
                kcn = min(8, KCQ - g8)
                kcg0 = kh * KCQ + g8
                srcs = [Mtb[:, kc2 * 128:(kc2 + 1) * 128]
                        for kc2 in range(g8, g8 + kcn)]
                emit_tgroup(srcs,
                            MTs[:, kcg0:kcg0 + kcn, nt * 128:(nt + 1) * 128],
                            "v", f"mtg{h}_{g8}", pre_waits=(f"mtb{h}",))
            ev[f"mtb_done{h}"] = ev[f"tp_mtg{h}_{(KCQ - 1) // 8 * 8}"]
            ev[f"q_done{h}"] = ev[f"cp_mtg{h}_{(KCQ - 1) // 8 * 8}"]

        qsched = {t: [] for t in range(NT)}
        cap = -(-NH // max(1, NT - 1))
        tile_of = {}
        prev_t = 1
        for h in range(NH):
            ntq = h % NTL
            t_assign = min(max(ntq, prev_t), NT - 1)
            while t_assign < NT - 1 and len(qsched[t_assign]) >= cap:
                t_assign += 1
            qsched[t_assign].append(h)
            tile_of[h] = t_assign
            prev_t = t_assign
        T_AG1 = max(tile_of[h] for h in range(2 * NTL))

        for t in range(NT):
            def _(sync, t=t):
                if t >= 2:
                    wait(sync, f"xnorm{t-2}")
                qX[t % 2].inc(sync.dma_start(out=xf[:, t % 2, :], in_=x3[:, t, :]), 16)
                done(f"x{t}", qX[t % 2])
            blk.sync(_)

            xtile = xf[:, t % 2, :]

            def _(s, t=t, xtile=xtile):
                wait(s, f"x{t}")
                if t == 0:
                    wait(s, "memsets")
                if t >= 2:
                    wait(s, f"zh{t-2}")
                cS.inc(s.activation(xnT[:, :, t * 128:(t + 1) * 128], xtile,
                                    AF.Copy, accum_out=st[:, 0:1]))
                s.drain()
                cS.inc(s.activation(xnT[:, :, t * 128:(t + 1) * 128], xtile,
                                    AF.Square, accum_out=st[:, 1:2]))
                s.drain()
                if t >= 1:
                    wait(s, f"xnorm{t-1}")
                cS.inc(s.mul(st[:, 2:3], st[:, 0:1], 1.0 / D))
                cS.inc(s.mul(st[:, 3:4], st[:, 1:2], 1.0 / D))
                s.drain()
                cS.inc(s.mul(st[:, 7:8], st[:, 2:3], st[:, 2:3]))
                s.drain()
                cS.inc(s.mul(st[:, 7:8], st[:, 7:8], -1.0))
                s.drain()
                cS.inc(s.add(st[:, 4:5], st[:, 3:4], st[:, 7:8]))
                s.drain()
                cS.inc(s.activation(st[:, 7:8], st[:, 4:5], AF.Sqrt,
                                    bias=epsc[:, 0:1]))
                smark(s, f"stats{t}")
            blk.scalar(_)

            def _(v, t=t, xtile=xtile):
                wait(v, f"stats{t}")
                if t >= 1:
                    wait(v, f"xn_done{t-1}")
                v.drain()
                cV.inc(v.reciprocal(st[:, 5:6], st[:, 7:8]))
                v.drain()
                cV.inc(v.tensor_scalar(st[:, 6:7], st[:, 2:3], st[:, 5:6], -1.0,
                                       ALU.mult, ALU.mult))
                v.drain()
                cV.inc(v.tensor_scalar(xnb[:, :], xtile, st[:, 5:6], st[:, 6:7],
                                       ALU.mult, ALU.add))
                vmark(v, f"xnorm{t}")
            blk.vector(_)

            NGRP = (KC + 7) // 8
            for gi in range(NGRP):
                kc0 = gi * 8
                kcn = min(8, KC - kc0)
                srcs = [xnb[:, kc * 128:(kc + 1) * 128]
                        for kc in range(kc0, kc0 + kcn)]
                emit_tgroup(srcs, xnT[:, kc0:kc0 + kcn, t * 128:(t + 1) * 128],
                            "v", f"xng{t}_{gi}",
                            pre_waits=(f"xnorm{t}",))
            ev[f"xn_done{t}"] = ev[f"tp_xng{t}_{NGRP-1}"]

            def _(t_, t=t, NGRP=NGRP):
                for gi in range(NGRP):
                    wait(t_, f"cp_xng{t}_{gi}")
                if t == 0:
                    wait(t_, "sconsts_v")
                    wait(t_, f"cp_w8g{(KC - 1) // 8 * 8}")
                    wait(t_, "memsets")
                if t >= 2:
                    wait(t_, f"zhfree{t % 2}")
                ztile = zh[t % 2]
                for kc in range(KC):
                    t_.matmul(ztile[:, 0:128], AW8b[:, kc, :],
                              xnT[:, kc, t * 128:(t + 1) * 128],
                              start=(kc == 0), stop=False)
                cT.inc(t_.matmul(ztile[:, 0:128], b8ext[:, :], ones128b[:, :],
                                 start=False, stop=True))
                done(f"zh{t}", cT)
            blk.tensor(_)

            def _(v, t=t):
                wait(v, f"zh{t}")
                cV.inc(v.tensor_copy(zT_sb[:, t * 128:(t + 1) * 128],
                                     zh[t % 2][0:R, 0:128]))
                cV.inc(v.tensor_copy(H8T_sb[:, t * 128:(t + 1) * 128],
                                     zh[t % 2][R:R + E, 0:128]))
                vmark(v, f"zcp{t}")
                done(f"hcp{t}", cV)
                done(f"zhfree{t % 2}", cV)
            blk.vector(_)

            if t < NTL:
                emit_mask_eighths(t)
                emit_bm_chain(t)
            for hq in qsched.get(t, []):
                emit_quarter(hq)

            if t == T_AG1:
                def _(g):
                    for h in range(2 * NTL):
                        wait(g, f"q_done{h}")
                    qD.inc(g.dma_start(out=ag_in1[:, :, :], in_=MTs[:, 0:KC2, :]), 16)
                    g.wait_ge(_qD, qD.n)
                    cc.inc(g.collective_compute(
                        "AllGather", ALU.bypass, replica_groups=RG,
                        ins=[ag_in1[:, :, :].opt()],
                        outs=[ag_out1[:, :, :, :].opt()]))
                    done("ag1", cc)
                blk.gpsimd(_)

        def _(v):
            for t in range(NT):
                wait(v, f"hcp{t}")
            cV.inc(v.reduce_max(mx2[:, 0:1], H8T_sb[:, 1:TPC:2], axis=AX.X,
                                apply_absolute_value=True))
            vmark(v, "hmax")
        blk.vector(_)

        def _(t_):
            for t in range(NT):
                wait(t_, f"zcp{t}")
            wait(t_, "cp_limt")
            wait(t_, "zhfree0"); wait(t_, "zhfree1")
            cT.inc(t_.matmul(zhA[0:E, 0:AN], LT_sb[:, :], zT_sb[:, 1:TPC:2],
                             start=True, stop=True))
            done("dTmm", cT)
        blk.tensor(_)

        def _(s):
            wait(s, "dTmm")
            cS.inc(s.copy(dT_sb[:, :], zhA[0:E, 0:AN]))
            smark(s, "dTcp")
        blk.scalar(_)

        def _(v):
            wait(v, "dTcp")
            cV.inc(v.reduce_max(mx2[:, 1:2], dT_sb[:, :], axis=AX.X,
                                apply_absolute_value=True))
            vmark(v, "dmax")
        blk.vector(_)

        def _(t_):
            wait(t_, "hmax"); wait(t_, "dmax")
            cT.inc(t_.transpose(zhB[0:2, 0:E], mx2[:, :], identf[0:E, 0:E]))
            done("mxT", cT)
        blk.tensor(_)

        def _(v):
            wait(v, "mxT")
            cV.inc(v.reduce_max(m21[:, :], zhB[0:2, 0:E], axis=AX.X))
            vmark(v, "m21")
        blk.vector(_)

        def _(t_):
            wait(t_, "m21")
            cT.inc(t_.transpose(zhA[0:1, 0:2], m21[:, :], identf[0:2, 0:2]))
            done("m21T", cT)
        blk.tensor(_)

        def _(v):
            wait(v, "m21T")
            cV.inc(v.memset(arst[:, :], 0.0))
            v.drain()
            cV.inc(v.tensor_copy(arst[0:1, 0:2], zhA[0:1, 0:2]))
            vmark(v, "arst")
        blk.vector(_)



        def _(g):
            for h in range(2 * NTL, NH):
                wait(g, f"q_done{h}")
            qD.inc(g.dma_start(out=ag_in2[:, :, :], in_=MTs[:, KC2:KC, :]), 16)
            g.wait_ge(_qD, qD.n)
            cc.inc(g.collective_compute(
                "AllGather", ALU.bypass, replica_groups=RG,
                ins=[ag_in2[:, :, :].opt()],
                outs=[ag_out2[:, :, :, :].opt()]))
            done("ag2", cc)
        blk.gpsimd(_)

        def _(g):
            wait(g, "arst")
            qC.inc(g.dma_start(out=ar_in[:, :], in_=arst[:, :]), 16)
            g.wait_ge(_qC, qC.n)
            cc.inc(g.collective_compute("AllReduce", ALU.max, replica_groups=RG,
                                        ins=[ar_in[:, :].opt()],
                                        outs=[ar_out[:, :].opt()]))
            done("ar", cc)
            g.wait_ge(_cc, cc.n)
            qC.inc(g.dma_start(out=row12[:, :], in_=ar_out[:, 0:2]), 16)
            done("scales_in", qC)
        blk.gpsimd(_)

        def _(v):
            wait(v, "scales_in")
            cV.inc(v.tensor_scalar_max(row12[:, :], row12[:, :], EPS_G))
            v.drain()
            cV.inc(v.reciprocal(row12[:, :], row12[:, :]))
            v.drain()
            cV.inc(v.tensor_scalar(row12[:, :], row12[:, :], GAMMA, None, ALU.mult))
            vmark(v, "scales")
        blk.vector(_)

        def _(t_):
            wait(t_, "scales")
            cT.inc(t_.matmul(zhB[0:E, 0:2], ones8f[:, :], row12[:, :],
                             start=True, stop=True))
            done("rHdmm", cT)
        blk.tensor(_)

        def _(v):
            wait(v, "rHdmm")
            cV.inc(v.tensor_copy(rHd[:, :], zhB[0:E, 0:2]))
            v.drain()
            cV.inc(v.tensor_scalar(ltb[:, :], dT_sb[:, :], rHd[:, 1:2],
                                   None, ALU.mult))
            v.drain()
            cV.inc(v.scalar_tensor_tensor(ltb[:, :], H8T_sb[:, 1:TPC:2],
                                          rHd[:, 0:1], ltb[:, :],
                                          ALU.mult, ALU.add))
            vmark(v, "ltb")
            done("rHd_done", cV)
        blk.vector(_)

        for a in range(NA):
            emit_tgroup([ltb[:, a * 128:(a + 1) * 128]],
                        garb[:, a * 8:(a + 1) * 8], "v",
                        f"gl{a}", pre_waits=("ltb",))

        def _(v):
            for a in range(NA):
                wait(v, f"cp_gl{a}")
            for a in range(NA):
                gl = garb[:, a * 8:(a + 1) * 8]
                cV.inc(v.reduce_max(garb[:, 64 + a:65 + a], gl, axis=AX.X,
                                    negate=True))
            vmark(v, "negm")
        blk.vector(_)

        def _(s):
            wait(s, "negm")
            for a in range(NA):
                cS.inc(s.activation(garb[:, 32 + a * 8:40 + a * 8],
                                    garb[:, a * 8:(a + 1) * 8], AF.Exp,
                                    bias=garb[:, 64 + a:65 + a]))
            smark(s, "gexp")
        blk.scalar(_)

        def _(v):
            wait(v, "gexp")
            m1pad = garb[:, 72:80]
            cV.inc(v.memset(garb[:, 73:80], -5.0))
            v.drain()
            for a in range(NA):
                ge = garb[:, 32 + a * 8:40 + a * 8]
                gp = garb[:, a * 8:(a + 1) * 8]
                ssum = garb[:, 68 + a:69 + a]
                th = garb[:, 80 + a:81 + a]
                cV.inc(v.reduce_sum(ssum, ge, axis=AX.X))
                v.drain()
                cV.inc(v.reciprocal(ssum, ssum))
                v.drain()
                cV.inc(v.tensor_scalar(gp, ge, ssum, None, ALU.mult))
                v.drain()
                cV.inc(v.reduce_max(garb[:, 72:73], gp, axis=AX.X))
                v.drain()
                cV.inc(v.match_replace(ge, m1pad, gp, -1e30))
                v.drain()
                cV.inc(v.reduce_max(th, ge, axis=AX.X))
                v.drain()
                cV.inc(v.tensor_scalar(th, th, -2.0, None, ALU.mult))
                v.drain()
            vmark(v, "gprobs")
        blk.vector(_)

        def _(s):
            wait(s, "gprobs")
            for a in range(NA):
                cS.inc(s.activation(garb[:, 32 + a * 8:40 + a * 8],
                                    garb[:, a * 8:(a + 1) * 8], AF.Sigmoid,
                                    bias=garb[:, 80 + a:81 + a], scale=2.0))
            smark(s, "gsig")
        blk.scalar(_)

        def _(v):
            wait(v, "gsig")
            for a in range(NA):
                gp = garb[:, a * 8:(a + 1) * 8]
                gs = garb[:, 32 + a * 8:40 + a * 8]
                s2 = garb[:, 68 + a:69 + a]
                cV.inc(v.tensor_tensor(gp, gp, gs, ALU.mult))
                v.drain()
                cV.inc(v.reduce_sum(s2, gp, axis=AX.X))
                v.drain()
                cV.inc(v.tensor_scalar(s2, s2, 1e-9, None, ALU.add))
                v.drain()
                cV.inc(v.reciprocal(s2, s2))
                v.drain()
                cV.inc(v.tensor_scalar(gbf[:, a * 8:(a + 1) * 8], gp, s2,
                                       None, ALU.mult))
                v.drain()
            vmark(v, "gwn")
        blk.vector(_)

        for a in range(NA):
            emit_tgroup([gbf[:, a * 8:(a + 1) * 8]], wnT_sb[:, :], "v",
                        f"wnT{a}", pre_waits=("gwn",))

            def _(v, a=a):
                wait(v, f"cp_wnT{a}")
                cV.inc(v.tensor_copy(AP(wfT, a * 256, [[TPC, E], [2, 128]]),
                                     wnT_sb[:, :]))
                cV.inc(v.tensor_copy(AP(wfT, a * 256 + 1, [[TPC, E], [2, 128]]),
                                     wnT_sb[:, :]))
                vmark(v, f"wfT{a}")
            blk.vector(_)

        def _(t_):
            for a in range(NA):
                wait(t_, f"wfT{a}")
            wait(t_, "rHd_done")
            for i, off in enumerate(range(0, TPC, 512)):
                wdt = min(512, TPC - off)
                cT.inc(t_.matmul(zh[i % 2][0:R, 0:wdt], LiMEsb[:, :],
                                 wfT[:, off:off + wdt], start=True, stop=True))
            done("pmix", cT)
        blk.tensor(_)

        def _(v):
            wait(v, "pmix")
            for i, off in enumerate(range(0, TPC, 512)):
                wdt = min(512, TPC - off)
                cV.inc(v.tensor_copy(pm_sb[:, off:off + wdt],
                                     zh[i % 2][0:R, 0:wdt]))
            v.drain()
            cV.inc(v.tensor_tensor(zpT[0:R, :], zT_sb[:, :], pm_sb[:, :], ALU.mult))
            vmark(v, "zpT")
        blk.vector(_)



        MTo_v = (xf[:, :, :].rearrange("p a d -> p (a d)").bitcast(BF16)
                 .rearrange("p (c n) -> p c n", n=NSL)[:, 0:KC, :])
        mtbuf = [MTs[:, :, :], MTo_v]
        for r in range(NCORES):
            def _(sync, r=r):
                wait(sync, "ag1")
                if r == 1:
                    wait(sync, f"xnorm{NT-1}")
                if r >= 2:
                    wait(sync, f"mm_r{r-2}_t{NT-1}")
                qM[r % 2].inc(sync.dma_start(out=mtbuf[r % 2][:, 0:KC2, :],
                                             in_=ag_out1[r, :, :, :]), 16)
                done(f"mtload{r}h1", qM[r % 2])
                wait(sync, "ag2")
                qM2[r % 2].inc(sync.dma_start(out=mtbuf[r % 2][:, KC2:KC, :],
                                              in_=ag_out2[r, :, :, :]), 16)
                done(f"mtload{r}h2", qM2[r % 2])
            blk.sync(_)

            for t in range(NT):
                it = r * NT + t
                DEF = 2
                NIT = NCORES * NT

                def _(t_, r=r, t=t, it=it, DEF=DEF, NIT=NIT):
                    wait(t_, f"mtload{r}h1")
                    if it >= 4:
                        itp = it - 4
                        wait(t_, f"ev_r{itp // NT}_t{itp % NT}")
                    if r == 0 and t == 0:
                        wait(t_, "W8b_Bext"); wait(t_, "Bext_row")
                    if it >= DEF:
                        itd = it - DEF
                        rd_, td_ = itd // NT, itd % NT
                        if itd == 0:
                            wait(t_, "zpT")
                        cT.inc(t_.matmul(pp[itd % 4][:, 0:NSL],
                                         zpT[:, td_ * 128:(td_ + 1) * 128],
                                         Bextb[:, rd_ * NSL:(rd_ + 1) * NSL],
                                         start=False, stop=True))
                        done(f"mm_r{rd_}_t{td_}", cT)
                    ppt = pp[it % 4]
                    for kc in range(KC2):
                        t_.matmul(ppt[:, 0:NSL], xnT[:, kc, t * 128:(t + 1) * 128],
                                  mtbuf[r % 2][:, kc, :], start=(kc == 0), stop=False)
                    wait(t_, f"mtload{r}h2")
                    for kc in range(KC2, KC):
                        ins = t_.matmul(ppt[:, 0:NSL], xnT[:, kc, t * 128:(t + 1) * 128],
                                        mtbuf[r % 2][:, kc, :], start=False, stop=False)
                    cT.inc(ins)
                    done(f"mmp_r{r}_t{t}", cT)
                    if it == NIT - 1:
                        for itd in range(NIT - DEF, NIT):
                            rd_, td_ = itd // NT, itd % NT
                            if itd == 0:
                                wait(t_, "zpT")
                            cT.inc(t_.matmul(pp[itd % 4][:, 0:NSL],
                                             zpT[:, td_ * 128:(td_ + 1) * 128],
                                             Bextb[:, rd_ * NSL:(rd_ + 1) * NSL],
                                             start=False, stop=True))
                            done(f"mm_r{rd_}_t{td_}", cT)
                blk.tensor(_)

                ready = [it - DEF] if it >= DEF else []
                for itd in ready:
                    rd_, td_ = itd // NT, itd % NT

                    def _(v, rd_=rd_, td_=td_, itd=itd):
                        wait(v, f"mm_r{rd_}_t{td_}")
                        wait(v, f"xr_{rd_}_{td_}")
                        if itd >= 3:
                            itp = itd - 3
                            wait(v, f"st_r{itp // NT}_t{itp % NT}")
                        cV.inc(v.tensor_tensor(stg[:, itd % 3, :],
                                               pp[itd % 4][:, 0:NSL],
                                               xr[:, itd % 2, :], ALU.add))
                        vmark(v, f"ev_r{rd_}_t{td_}")
                    blk.vector(_)

                    def _(g, rd_=rd_, td_=td_, itd=itd):
                        wait(g, f"ev_r{rd_}_t{td_}")
                        sx = sS[itd % 3]
                        sx.inc(g.dma_start(
                            out=out3[:, td_, rd_ * NSL:(rd_ + 1) * NSL],
                            in_=stg[:, itd % 3, :]), 16)
                        done(f"st_r{rd_}_t{td_}", sx)
                    blk.gpsimd(_)

                def _(sync, r=r, t=t, it=it):
                    if it >= 2:
                        itp = it - 2
                        wait(sync, f"ev_r{itp // NT}_t{itp % NT}")
                    qR[it % 2].inc(sync.dma_start(out=xr[:, it % 2, :],
                                   in_=x3[:, t, r * NSL:(r + 1) * NSL]), 16)
                    done(f"xr_{r}_{t}", qR[it % 2])
                blk.sync(_)

        NIT = NCORES * NT
        for itd in range(NIT - 2, NIT):
            rd_, td_ = itd // NT, itd % NT

            def _(v, rd_=rd_, td_=td_, itd=itd):
                wait(v, f"mm_r{rd_}_t{td_}")
                wait(v, f"xr_{rd_}_{td_}")
                if itd >= 3:
                    itp = itd - 3
                    wait(v, f"st_r{itp // NT}_t{itp % NT}")
                cV.inc(v.tensor_tensor(stg[:, itd % 3, :],
                                       pp[itd % 4][:, 0:NSL],
                                       xr[:, itd % 2, :], ALU.add))
                vmark(v, f"ev_r{rd_}_t{td_}")
            blk.vector(_)

            def _(g, rd_=rd_, td_=td_, itd=itd):
                wait(g, f"ev_r{rd_}_t{td_}")
                sx = sS[itd % 3]
                sx.inc(g.dma_start(
                    out=out3[:, td_, rd_ * NSL:(rd_ + 1) * NSL],
                    in_=stg[:, itd % 3, :]), 16)
                done(f"st_r{rd_}_t{td_}", sx)
            blk.gpsimd(_)

        def _(g):
            for scnt in sS:
                g.wait_ge(scnt.sem, scnt.n)
        blk.gpsimd(_)

    return nc


def shard_inputs(data, mask, weight, bias, W, b, A, Bmat, LiMEs, TPC, D):
    xflat = np.ascontiguousarray(np.asarray(data, np.float32).reshape(-1, D))
    mask = np.asarray(mask, np.int32)
    weight = np.asarray(weight, np.float32)
    W = np.asarray(W, np.float32)
    NSL = D // NCORES
    identb = np.eye(128, dtype=ml_dtypes.bfloat16)
    identf = np.eye(128, dtype=np.float32)
    E32 = np.zeros((128, 4), dtype=ml_dtypes.bfloat16)
    for n in range(128):
        E32[n, n // 32] = 1
    Emat = np.ascontiguousarray(E32.T)
    in_maps = []
    for c in range(NCORES):
        in_maps.append({
            "x": np.ascontiguousarray(xflat[c * TPC:(c + 1) * TPC]),
            "mask_sl": np.ascontiguousarray(mask[c * NSL:(c + 1) * NSL]),
            "w_sl": np.ascontiguousarray(weight[c * NSL:(c + 1) * NSL]),
            "Wf_sl": np.ascontiguousarray(W[c * NSL:(c + 1) * NSL]),
            "W8": np.ascontiguousarray(W[0:E]),
            "biasrow": np.asarray(bias, np.float32).reshape(1, D),
            "brow": np.asarray(b, np.float32).reshape(1, D),
            "A": np.asarray(A, np.float32),
            "Bmat": np.asarray(Bmat, np.float32),
            "LiMEs": np.asarray(LiMEs, np.float32),
            "identb": identb,
            "identf": identf,
            "E32": E32,
            "Emat": Emat,
        })
    return in_maps



from concourse.bass_utils import run_bass_kernel_spmd

TPC_REAL = 1024
D_REAL = 4096


def kernel(L0_attn_data, L0_attn_mask, L0_attn_weight, L0_attn_bias,
           L0_ffn_W, L0_ffn_b, L0_ffn_A, L0_ffn_Bmat, L0_ffn_LiMEs):
    TPC, D = TPC_REAL, D_REAL
    nc = build_program(TPC, D, detect_races=True)
    in_maps = shard_inputs(L0_attn_data, L0_attn_mask, L0_attn_weight,
                           L0_attn_bias, L0_ffn_W, L0_ffn_b, L0_ffn_A,
                           L0_ffn_Bmat, L0_ffn_LiMEs, TPC, D)
    res = run_bass_kernel_spmd(nc, in_maps, core_ids=list(range(NCORES)))
    out = np.concatenate([np.asarray(r["out"]) for r in res.results], axis=0)
    return out.reshape(4, 2048, D).astype(np.float32, copy=False)


# version 14
# speedup vs baseline: 1.0535x; 1.0085x over previous
import numpy as np
import ml_dtypes
import concourse.bass as bass
import concourse.mybir as mybir
from concourse.bass_types import AP

F32 = mybir.dt.float32
BF16 = mybir.dt.bfloat16
I32 = mybir.dt.int32
AF = mybir.ActivationFunctionType
ALU = mybir.AluOpType
AX = mybir.AxisListType

NCORES = 8
R = 32
E = 8
BW = 32
EPS_LN = 1e-5
EPS_G = 1e-6
GAMMA = 0.5
SCALING = 0.5


class Ctr:
    def __init__(self, sem):
        self.sem = sem
        self.n = 0

    def inc(self, instr, k=1):
        instr.then_inc(self.sem, k)
        self.n += k
        return self.n


def build_program(TPC, D, detect_races=True):
    NT = TPC // 128
    KC = D // 128
    KC2 = KC // 2
    NSL = D // NCORES
    KQ = D // 4
    KCQ = KC // 4
    NTL = NSL // 128
    KE = KQ // 2
    KBE = KE // BW
    AN = TPC // 2
    NA = AN // 128
    NH = 4 * NTL
    assert TPC % 256 == 0 and D % 256 == 0 and NSL % 128 == 0

    nc = bass.Bass("TRN2", target_bir_lowering=False,
                   detect_race_conditions=detect_races)

    x_in = nc.dram_tensor("x", [TPC, D], F32, kind="ExternalInput")
    mask_sl = nc.dram_tensor("mask_sl", [NSL, D], I32, kind="ExternalInput")
    w_sl = nc.dram_tensor("w_sl", [NSL, D], F32, kind="ExternalInput")
    Wf_sl = nc.dram_tensor("Wf_sl", [NSL, D], F32, kind="ExternalInput")
    W8_in = nc.dram_tensor("W8", [E, D], F32, kind="ExternalInput")
    biasrow_in = nc.dram_tensor("biasrow", [1, D], F32, kind="ExternalInput")
    brow_in = nc.dram_tensor("brow", [1, D], F32, kind="ExternalInput")
    A_in = nc.dram_tensor("A", [D, R], F32, kind="ExternalInput")
    Bmat_in = nc.dram_tensor("Bmat", [R, D], F32, kind="ExternalInput")
    LiMEs_in = nc.dram_tensor("LiMEs", [E, R], F32, kind="ExternalInput")
    identb_in = nc.dram_tensor("identb", [128, 128], BF16, kind="ExternalInput")
    identf_in = nc.dram_tensor("identf", [128, 128], F32, kind="ExternalInput")
    E32_in = nc.dram_tensor("E32", [128, 4], BF16, kind="ExternalInput")
    Emat_in = nc.dram_tensor("Emat", [4, 128], BF16, kind="ExternalInput")
    out_d = nc.dram_tensor("out", [TPC, D], F32, kind="ExternalOutput")

    ag_in1 = nc.dram_tensor("ag_in1", [128, KC2, NSL], BF16)
    ag_in2 = nc.dram_tensor("ag_in2", [128, KC2, NSL], BF16)
    ag_out1 = nc.dram_tensor("ag_out1", [NCORES, 128, KC2, NSL], BF16,
                             addr_space="Shared")
    ag_out2 = nc.dram_tensor("ag_out2", [NCORES, 128, KC2, NSL], BF16,
                             addr_space="Shared")
    ar_in = nc.dram_tensor("ar_in", [1, 8], F32)
    ar_out = nc.dram_tensor("ar_out", [1, 8], F32)
    RG = [list(range(NCORES))]

    x3 = x_in[:, :].rearrange("(t p) d -> p t d", p=128)
    out3 = out_d[:, :].rearrange("(t p) d -> p t d", p=128)

    from contextlib import ExitStack
    ES = ExitStack()
    if True:
        xnT = ES.enter_context(nc.sbuf_tensor("xnT", [128, KC, TPC], BF16))
        AW8b = ES.enter_context(nc.sbuf_tensor("AW8b", [128, KC, R + E], BF16))
        zT_sb = ES.enter_context(nc.sbuf_tensor("zT_sb", [R, TPC], BF16))
        H8T_sb = ES.enter_context(nc.sbuf_tensor("H8T_sb", [E, TPC], BF16))
        dT_sb = ES.enter_context(nc.sbuf_tensor("dT_sb", [E, AN], BF16))
        ltb = ES.enter_context(nc.sbuf_tensor("ltb", [E, AN], BF16))
        wfT = ES.enter_context(nc.sbuf_tensor("wfT", [E, TPC], BF16))
        pm_sb = ES.enter_context(nc.sbuf_tensor("pm_sb", [R, TPC], BF16))
        zpT = ES.enter_context(nc.sbuf_tensor("zpT", [R + 1, TPC], BF16))
        Bextb = ES.enter_context(nc.sbuf_tensor("Bextb", [R + 1, D], BF16))
        b8ext = ES.enter_context(nc.sbuf_tensor("b8ext", [1, R + E], BF16))
        ones128b = ES.enter_context(nc.sbuf_tensor("ones128b", [1, 128], BF16))
        ones8f = ES.enter_context(nc.sbuf_tensor("ones8f", [1, E], F32))
        LiMEsb = ES.enter_context(nc.sbuf_tensor("LiMEsb", [E, R], BF16))
        LT_sb = ES.enter_context(nc.sbuf_tensor("LT_sb", [R, E], BF16))
        identb = ES.enter_context(nc.sbuf_tensor("identb_s", [128, 128], BF16))
        identf = ES.enter_context(nc.sbuf_tensor("identf_s", [128, 128], F32))
        E32b = ES.enter_context(nc.sbuf_tensor("E32b_s", [128, 4], BF16))
        Ematb = ES.enter_context(nc.sbuf_tensor("Ematb", [4, 128], BF16))
        st = ES.enter_context(nc.sbuf_tensor("stats", [128, 8], F32))
        epsc = ES.enter_context(nc.sbuf_tensor("epsc", [128, 1], F32))
        mx2 = ES.enter_context(nc.sbuf_tensor("mx2", [E, 2], F32))
        m21 = ES.enter_context(nc.sbuf_tensor("m21", [2, 1], F32))
        row12 = ES.enter_context(nc.sbuf_tensor("row12", [1, 2], F32))
        arst = ES.enter_context(nc.sbuf_tensor("arst", [1, 8], F32))
        rHd = ES.enter_context(nc.sbuf_tensor("rHd", [E, 2], F32))
        garb = ES.enter_context(nc.sbuf_tensor("garb", [128, 84], F32))
        gbf = ES.enter_context(nc.sbuf_tensor("gbf", [128, 32], BF16))
        wnT_sb = ES.enter_context(nc.sbuf_tensor("wnT_sb", [E, 128], BF16))
        W8b = ES.enter_context(nc.sbuf_tensor("W8b", [E, D], BF16))
        xf = ES.enter_context(nc.sbuf_tensor("xf", [128, 2, D], F32))
        xnb = ES.enter_context(nc.sbuf_tensor("xnb", [128, D], BF16))
        maskb = ES.enter_context(nc.sbuf_tensor("maskb", [128, 2, KQ], I32))
        wb = ES.enter_context(nc.sbuf_tensor("wb", [128, 2, KQ], F32))
        Wbb = ES.enter_context(nc.sbuf_tensor("Wbb", [128, 2, KQ], BF16))
        red1w = ES.enter_context(nc.sbuf_tensor("red1w", [128, D // BW // 4 * 4], BF16))
        bm4T = ES.enter_context(nc.sbuf_tensor("bm4T", [4, D // BW], BF16))
        bmex_all = ES.enter_context(nc.sbuf_tensor("bmex_all", [128, NTL, D // BW], F32))
        Mtb = ES.enter_context(nc.sbuf_tensor("Mtb", [128, KQ], BF16))
        MTs = ES.enter_context(nc.sbuf_tensor("MTs", [128, KC, NSL], BF16))
        xr = ES.enter_context(nc.sbuf_tensor("xr", [128, 2, NSL], F32))
        stg = ES.enter_context(nc.sbuf_tensor("stg", [128, 3, NSL], F32))
        ppA = ES.enter_context(nc.psum_tensor("ppA", [128, 512], F32))
        ppB = ES.enter_context(nc.psum_tensor("ppB", [128, 512], F32))
        zhA = ES.enter_context(nc.psum_tensor("zhA", [R + E, 512], F32))
        zhB = ES.enter_context(nc.psum_tensor("zhB", [R + E, 512], F32))
        ptA = ES.enter_context(nc.psum_tensor("ptA", [128, 1024], BF16))
        ptB = ES.enter_context(nc.psum_tensor("ptB", [128, 1024], BF16))
        pfA = ES.enter_context(nc.psum_tensor("pfA", [128, 512], F32))
        pfB = ES.enter_context(nc.psum_tensor("pfB", [128, 512], F32))
        _qA = ES.enter_context(nc.semaphore("qA"))
        _qG = ES.enter_context(nc.semaphore("qG"))
        _qB = ES.enter_context(nc.semaphore("qB"))
        _qC = ES.enter_context(nc.semaphore("qC"))
        _qD = ES.enter_context(nc.semaphore("qD"))
        _cT = ES.enter_context(nc.semaphore("cT"))
        _cV = ES.enter_context(nc.semaphore("cV"))
        _cS = ES.enter_context(nc.semaphore("cS"))
        _cG = ES.enter_context(nc.semaphore("cG"))
        _cc = ES.enter_context(nc.semaphore("cc"))
        _qX0 = ES.enter_context(nc.semaphore("qX0"))
        _qX1 = ES.enter_context(nc.semaphore("qX1"))
        _qM0 = ES.enter_context(nc.semaphore("qM0"))
        _qM1 = ES.enter_context(nc.semaphore("qM1"))
        _qM2 = ES.enter_context(nc.semaphore("qM2"))
        _qM3 = ES.enter_context(nc.semaphore("qM3"))
        _qR0 = ES.enter_context(nc.semaphore("qR0"))
        _qR1 = ES.enter_context(nc.semaphore("qR1"))
        _qW0 = ES.enter_context(nc.semaphore("qW0"))
        _qW1 = ES.enter_context(nc.semaphore("qW1"))
        _qE0 = ES.enter_context(nc.semaphore("qE0"))
        _qE1 = ES.enter_context(nc.semaphore("qE1"))
        _qF = ES.enter_context(nc.semaphore("qF"))
        _s0 = ES.enter_context(nc.semaphore("s0"))
        _s1 = ES.enter_context(nc.semaphore("s1"))
        _s2 = ES.enter_context(nc.semaphore("s2"))
        blk = ES.enter_context(nc.Block())
    with ES:
        qA, qG, qB = Ctr(_qA), Ctr(_qG), Ctr(_qB)
        qC, qD = Ctr(_qC), Ctr(_qD)
        cT, cV, cS, cG, cc = Ctr(_cT), Ctr(_cV), Ctr(_cS), Ctr(_cG), Ctr(_cc)
        qX = [Ctr(_qX0), Ctr(_qX1)]
        qM = [Ctr(_qM0), Ctr(_qM1)]
        qM2 = [Ctr(_qM2), Ctr(_qM3)]
        qR = [Ctr(_qR0), Ctr(_qR1)]
        qW = [Ctr(_qW0), Ctr(_qW1)]
        qE = [Ctr(_qE0), Ctr(_qE1)]
        qF = Ctr(_qF)
        sS = [Ctr(_s0), Ctr(_s1), Ctr(_s2)]
        pp = [ppA, ppB, pfA, pfB]
        zh = [zhA, zhB]
        pt = [ptA, ptB]
        ev = {}

        def done(name, ctr):
            ev[name] = (ctr, ctr.n)

        def wait(eng, name):
            c, v = ev[name]
            if v > 0:
                eng.wait_ge(c.sem, v)

        def vmark(v, name):
            cV.inc(v.drain())
            done(name, cV)

        def smark(s, name):
            cS.inc(s.drain())
            done(name, cS)

        tpar = [0]

        def emit_tgroup(srcs, dst_ap, copy_engine, name, pre_waits=()):
            par = tpar[0] % 2
            tpar[0] += 1
            prev = ev.get(f"ptfree{par}")
            w_out = srcs[0].shape[1]

            def _t(t):
                for wn_ in pre_waits:
                    wait(t, wn_)
                if prev is not None and prev[1] > 0:
                    t.wait_ge(prev[0].sem, prev[1])
                off = 0
                for i, sap in enumerate(srcs):
                    p_in = sap.shape[0]
                    ins = t.transpose(pt[par][0:w_out, off:off + p_in], sap,
                                      identb[0:p_in, 0:p_in])
                    if i == len(srcs) - 1:
                        cT.inc(ins)
                    off += p_in
                done(f"tp_{name}", cT)
            blk.tensor(_t)

            def _c(c):
                wait(c, f"tp_{name}")
                total = sum(sap.shape[0] for sap in srcs)
                srcp = pt[par][0:w_out, 0:total]
                if dst_ap.ndim == 3:
                    srcp = srcp.rearrange("p (a b) -> p a b", b=dst_ap.shape[2])
                if copy_engine == "v":
                    cV.inc(c.tensor_copy(dst_ap, srcp))
                    done(f"ptfree{par}", cV)
                    done(f"cp_{name}", cV)
                else:
                    cS.inc(c.copy(dst_ap, srcp))
                    done(f"ptfree{par}", cS)
                    done(f"cp_{name}", cS)
            (blk.vector if copy_engine == "v" else blk.scalar)(_c)

        def _(g):
            qG.inc(g.dma_start(out=AW8b[:, :, 0:R],
                               in_=A_in[:, :].rearrange("(c k) r -> k c r", k=128)), 16)
            qG.inc(g.dma_start(out=W8b[:, :], in_=W8_in[:, :]), 16)
            qG.inc(g.dma_start(out=LiMEsb[:, :], in_=LiMEs_in[:, :]), 16)

            qG.inc(g.dma_start(out=Bextb[0:R, :], in_=Bmat_in[:, :]), 16)
            qG.inc(g.dma_start(out=identb[:, :], in_=identb_in[:, :]), 16)
            qG.inc(g.dma_start(out=identf[:, :], in_=identf_in[:, :]), 16)
            qG.inc(g.dma_start(out=E32b[:, :], in_=E32_in[:, :]), 16)
            qG.inc(g.dma_start(out=Ematb[:, :], in_=Emat_in[:, :]), 16)
            for nm in ("smalls", "consts"):
                done(nm, qG)
            qB.inc(g.dma_start(out=Bextb[R:R + 1, :], in_=brow_in[:, :]), 16)
            g.wait_ge(_qB, 16)
            qB.inc(g.dma_start(out=Bextb[R:R + 1, :], in_=biasrow_in[:, :],
                               accum_op=ALU.add), 16)
            cG.inc(g.memset(ones128b[:, :], 1.0))
            cG.inc(g.memset(ones8f[:, :], 1.0))
            cG.inc(g.memset(b8ext[:, :], 0.0))
            cG.inc(g.memset(zpT[R:R + 1, :], 1.0))
            cG.inc(g.memset(epsc[:, :], EPS_LN))
            done("memsets", cG)
            g.wait_ge(_cG, cG.n)
            qF.inc(g.dma_start(out=b8ext[0:1, R:R + E], in_=brow_in[0:1, 0:E],
                               allow_slow_non_contiguous=True), 16)
            done("b8l", qF)
        blk.gpsimd(_)
        ev["Bext_row"] = (qB, 32)

        ev["sconsts_v"] = ev["b8l"]

        def _(s):
            wait(s, "smalls")
            cS.inc(s.mul(Bextb[0:R, :], Bextb[0:R, :], SCALING))
            smark(s, "W8b_Bext")
        blk.scalar(_)

        for g8 in range(0, KC, 8):
            kcn = min(8, KC - g8)
            srcs = [W8b[:, kc * 128:(kc + 1) * 128] for kc in range(g8, g8 + kcn)]
            emit_tgroup(srcs, AW8b[:, g8:g8 + kcn, R:R + E], "v", f"w8g{g8}",
                        pre_waits=("smalls", "consts"))
        emit_tgroup([LiMEsb[:, :]], LT_sb[:, :], "v", "limt",
                    pre_waits=("sconsts_v",))

        KB = D // BW

        def emit_mask_eighths(nt):
            KBQ4 = KQ // BW

            def emit_mload(nt, e):
                def _(g, nt=nt, e=e):
                    m = nt * 4 + e
                    if m >= 2:
                        pm = m - 2
                        wait(g, f"red1_{pm // 4}_{pm % 4}")
                    if nt > 0 and e < 2:
                        wait(g, f"bsum{nt-1}")
                    ks = e * KQ
                    qE[e % 2].inc(g.dma_start(out=maskb[:, e % 2, :],
                                  in_=mask_sl[nt * 128:nt * 128 + 128,
                                              ks:ks + KQ]), 16)
                    done(f"mload{nt}_{e}", qE[e % 2])
                blk.gpsimd(_)

            def emit_mred(nt, e):
                def _(v, nt=nt, e=e):
                    wait(v, f"mload{nt}_{e}")
                    if nt > 0 and e == 0:
                        wait(v, f"bsum{nt-1}")
                    with nc.allow_low_precision(reason="mask block sums <=32 exact in bf16"):
                        cV.inc(v.reduce_sum(
                            red1w[:, e * KBQ4:(e + 1) * KBQ4],
                            maskb[:, e % 2, :].rearrange("p (g j) -> p g j", j=BW),
                            axis=AX.X))
                    vmark(v, f"red1_{nt}_{e}")
                blk.vector(_)

            emit_mload(nt, 0)
            emit_mload(nt, 1)
            for e in range(4):
                emit_mred(nt, e)
                if e + 2 < 4:
                    emit_mload(nt, e + 2)

        def emit_bm_chain(nt):
            def _(t_, nt=nt):
                for e in range(4):
                    wait(t_, f"red1_{nt}_{e}")
                if nt > 0:
                    wait(t_, f"bm01_{nt-1}")
                cT.inc(t_.matmul(pfA[0:4, 0:KB], E32b[:, :], red1w[:, 0:KB],
                                 start=True, stop=True))
                done(f"bsum{nt}", cT)
            blk.tensor(_)

            def _(v, nt=nt):
                wait(v, f"bsum{nt}")
                if nt > 0:
                    wait(v, f"bexp{nt-1}")
                cV.inc(v.tensor_single_scalar(bm4T[:, 0:KB], pfA[0:4, 0:KB],
                                              0.0, ALU.is_gt))
                vmark(v, f"bm01_{nt}")
            blk.vector(_)

            def _(t_, nt=nt):
                wait(t_, f"bm01_{nt}")
                if nt > 0:
                    wait(t_, f"bmex_{nt-1}")
                cT.inc(t_.matmul(pfB[:, 0:KB], Ematb[:, :], bm4T[:, 0:KB],
                                 start=True, stop=True))
                done(f"bexp{nt}", cT)
            blk.tensor(_)

            def _(v, nt=nt):
                wait(v, f"bexp{nt}")
                cV.inc(v.tensor_copy(bmex_all[:, nt, :], pfB[:, 0:KB]))
                vmark(v, f"bmex_{nt}")
            blk.vector(_)

        def emit_quarter(h):
            kh, nt = h // NTL, h % NTL
            rs, ks = nt * 128, kh * KQ
            kbq = KQ // BW

            def _(g, h=h, rs=rs, ks=ks):
                if h > 1:
                    wait(g, f"mtb{h-2}")
                qW[h % 2].inc(g.dma_start(out=wb[:, h % 2, :],
                              in_=w_sl[rs:rs + 128, ks:ks + KQ]), 16)
                qW[h % 2].inc(g.dma_start(out=Wbb[:, h % 2, :],
                              in_=Wf_sl[rs:rs + 128, ks:ks + KQ]), 16)
                done(f"wload{h}", qW[h % 2])
            blk.gpsimd(_)

            def _(v, h=h, kh=kh, nt=nt, kbq=kbq):
                wait(v, f"wload{h}")
                wait(v, f"bmex_{nt}")
                if h > 0:
                    wait(v, f"mtb_done{h-1}")
                w3i = AP(wb, (h % 2) * KQ, [[2 * KQ, 128], [BW, kbq], [1, BW]])
                w3o = AP(Mtb, 0, [[KQ, 128], [BW, kbq], [1, BW]])
                b3 = AP(bmex_all, nt * KB + kh * kbq,
                        [[NTL * KB, 128], [1, kbq], [0, BW]])
                cV.inc(v.tensor_tensor(w3o, w3i, b3, ALU.mult))
                v.drain()
                cV.inc(v.tensor_tensor(Mtb[:, :], Mtb[:, :], Wbb[:, h % 2, :],
                                       ALU.add))
                vmark(v, f"mtb{h}")
            blk.vector(_)

            for g8 in range(0, KCQ, 8):
                kcn = min(8, KCQ - g8)
                kcg0 = kh * KCQ + g8
                srcs = [Mtb[:, kc2 * 128:(kc2 + 1) * 128]
                        for kc2 in range(g8, g8 + kcn)]
                emit_tgroup(srcs,
                            MTs[:, kcg0:kcg0 + kcn, nt * 128:(nt + 1) * 128],
                            "v", f"mtg{h}_{g8}", pre_waits=(f"mtb{h}",))
            ev[f"mtb_done{h}"] = ev[f"tp_mtg{h}_{(KCQ - 1) // 8 * 8}"]
            ev[f"q_done{h}"] = ev[f"cp_mtg{h}_{(KCQ - 1) // 8 * 8}"]

        qsched = {t: [] for t in range(NT)}
        cap = -(-NH // max(1, NT - 1))
        tile_of = {}
        prev_t = 1
        for h in range(NH):
            ntq = h % NTL
            t_assign = min(max(ntq, prev_t), NT - 1)
            while t_assign < NT - 1 and len(qsched[t_assign]) >= cap:
                t_assign += 1
            qsched[t_assign].append(h)
            tile_of[h] = t_assign
            prev_t = t_assign
        T_AG1 = max(tile_of[h] for h in range(2 * NTL))

        for t in range(NT):
            def _(sync, t=t):
                if t >= 2:
                    wait(sync, f"xnorm{t-2}")
                qX[t % 2].inc(sync.dma_start(out=xf[:, t % 2, :], in_=x3[:, t, :]), 16)
                done(f"x{t}", qX[t % 2])
            blk.sync(_)

            xtile = xf[:, t % 2, :]

            def _(s, t=t, xtile=xtile):
                wait(s, f"x{t}")
                if t == 0:
                    wait(s, "memsets")
                if t >= 2:
                    wait(s, f"zh{t-2}")
                cS.inc(s.activation(xnT[:, :, t * 128:(t + 1) * 128], xtile,
                                    AF.Copy, accum_out=st[:, 0:1]))
                s.drain()
                cS.inc(s.activation(xnT[:, :, t * 128:(t + 1) * 128], xtile,
                                    AF.Square, accum_out=st[:, 1:2]))
                s.drain()
                if t >= 1:
                    wait(s, f"xnorm{t-1}")
                cS.inc(s.mul(st[:, 2:3], st[:, 0:1], 1.0 / D))
                cS.inc(s.mul(st[:, 3:4], st[:, 1:2], 1.0 / D))
                s.drain()
                cS.inc(s.mul(st[:, 7:8], st[:, 2:3], st[:, 2:3]))
                s.drain()
                cS.inc(s.mul(st[:, 7:8], st[:, 7:8], -1.0))
                s.drain()
                cS.inc(s.add(st[:, 4:5], st[:, 3:4], st[:, 7:8]))
                s.drain()
                cS.inc(s.activation(st[:, 7:8], st[:, 4:5], AF.Sqrt,
                                    bias=epsc[:, 0:1]))
                smark(s, f"stats{t}")
            blk.scalar(_)

            def _(v, t=t, xtile=xtile):
                wait(v, f"stats{t}")
                if t >= 1:
                    wait(v, f"xn_done{t-1}")
                v.drain()
                cV.inc(v.reciprocal(st[:, 5:6], st[:, 7:8]))
                v.drain()
                cV.inc(v.tensor_scalar(st[:, 6:7], st[:, 2:3], st[:, 5:6], -1.0,
                                       ALU.mult, ALU.mult))
                v.drain()
                cV.inc(v.tensor_scalar(xnb[:, :], xtile, st[:, 5:6], st[:, 6:7],
                                       ALU.mult, ALU.add))
                vmark(v, f"xnorm{t}")
            blk.vector(_)

            NGRP = (KC + 7) // 8
            for gi in range(NGRP):
                kc0 = gi * 8
                kcn = min(8, KC - kc0)
                srcs = [xnb[:, kc * 128:(kc + 1) * 128]
                        for kc in range(kc0, kc0 + kcn)]
                emit_tgroup(srcs, xnT[:, kc0:kc0 + kcn, t * 128:(t + 1) * 128],
                            "v", f"xng{t}_{gi}",
                            pre_waits=(f"xnorm{t}",))
            ev[f"xn_done{t}"] = ev[f"tp_xng{t}_{NGRP-1}"]

            def _(t_, t=t, NGRP=NGRP):
                for gi in range(NGRP):
                    wait(t_, f"cp_xng{t}_{gi}")
                if t == 0:
                    wait(t_, "sconsts_v")
                    wait(t_, f"cp_w8g{(KC - 1) // 8 * 8}")
                    wait(t_, "memsets")
                if t >= 2:
                    wait(t_, f"zhfree{t % 2}")
                ztile = zh[t % 2]
                for kc in range(KC):
                    t_.matmul(ztile[:, 0:128], AW8b[:, kc, :],
                              xnT[:, kc, t * 128:(t + 1) * 128],
                              start=(kc == 0), stop=False)
                cT.inc(t_.matmul(ztile[:, 0:128], b8ext[:, :], ones128b[:, :],
                                 start=False, stop=True))
                done(f"zh{t}", cT)
            blk.tensor(_)

            def _(v, t=t):
                wait(v, f"zh{t}")
                cV.inc(v.tensor_copy(zT_sb[:, t * 128:(t + 1) * 128],
                                     zh[t % 2][0:R, 0:128]))
                cV.inc(v.tensor_copy(H8T_sb[:, t * 128:(t + 1) * 128],
                                     zh[t % 2][R:R + E, 0:128]))
                vmark(v, f"zcp{t}")
                done(f"hcp{t}", cV)
                done(f"zhfree{t % 2}", cV)
            blk.vector(_)

            if t < NTL:
                emit_mask_eighths(t)
                emit_bm_chain(t)
            for hq in qsched.get(t, []):
                emit_quarter(hq)

            if t == T_AG1:
                def _(g):
                    for h in range(2 * NTL):
                        wait(g, f"q_done{h}")
                    qD.inc(g.dma_start(out=ag_in1[:, :, :], in_=MTs[:, 0:KC2, :]), 16)
                    g.wait_ge(_qD, qD.n)
                    cc.inc(g.collective_compute(
                        "AllGather", ALU.bypass, replica_groups=RG,
                        ins=[ag_in1[:, :, :].opt()],
                        outs=[ag_out1[:, :, :, :].opt()]))
                    done("ag1", cc)
                blk.gpsimd(_)

        def _(v):
            for t in range(NT):
                wait(v, f"hcp{t}")
            cV.inc(v.reduce_max(mx2[:, 0:1], H8T_sb[:, 1:TPC:2], axis=AX.X,
                                apply_absolute_value=True))
            vmark(v, "hmax")
        blk.vector(_)

        def _(t_):
            for t in range(NT):
                wait(t_, f"zcp{t}")
            wait(t_, "cp_limt")
            wait(t_, "zhfree0"); wait(t_, "zhfree1")
            cT.inc(t_.matmul(zhA[0:E, 0:AN], LT_sb[:, :], zT_sb[:, 1:TPC:2],
                             start=True, stop=True))
            done("dTmm", cT)
        blk.tensor(_)

        def _(s):
            wait(s, "dTmm")
            cS.inc(s.copy(dT_sb[:, :], zhA[0:E, 0:AN]))
            smark(s, "dTcp")
        blk.scalar(_)

        def _(v):
            wait(v, "dTcp")
            cV.inc(v.reduce_max(mx2[:, 1:2], dT_sb[:, :], axis=AX.X,
                                apply_absolute_value=True))
            vmark(v, "dmax")
        blk.vector(_)

        def _(t_):
            wait(t_, "hmax"); wait(t_, "dmax")
            cT.inc(t_.transpose(zhB[0:2, 0:E], mx2[:, :], identf[0:E, 0:E]))
            done("mxT", cT)
        blk.tensor(_)

        def _(v):
            wait(v, "mxT")
            cV.inc(v.reduce_max(m21[:, :], zhB[0:2, 0:E], axis=AX.X))
            vmark(v, "m21")
        blk.vector(_)

        def _(t_):
            wait(t_, "m21")
            cT.inc(t_.transpose(zhA[0:1, 0:2], m21[:, :], identf[0:2, 0:2]))
            done("m21T", cT)
        blk.tensor(_)

        def _(v):
            wait(v, "m21T")
            cV.inc(v.memset(arst[:, :], 0.0))
            v.drain()
            cV.inc(v.tensor_copy(arst[0:1, 0:2], zhA[0:1, 0:2]))
            vmark(v, "arst")
        blk.vector(_)



        def _(g):
            for h in range(2 * NTL, NH):
                wait(g, f"q_done{h}")
            qD.inc(g.dma_start(out=ag_in2[:, :, :], in_=MTs[:, KC2:KC, :]), 16)
            g.wait_ge(_qD, qD.n)
            cc.inc(g.collective_compute(
                "AllGather", ALU.bypass, replica_groups=RG,
                ins=[ag_in2[:, :, :].opt()],
                outs=[ag_out2[:, :, :, :].opt()]))
            done("ag2", cc)
        blk.gpsimd(_)

        def _(g):
            wait(g, "arst")
            qC.inc(g.dma_start(out=ar_in[:, :], in_=arst[:, :]), 16)
            g.wait_ge(_qC, qC.n)
            cc.inc(g.collective_compute("AllReduce", ALU.max, replica_groups=RG,
                                        ins=[ar_in[:, :].opt()],
                                        outs=[ar_out[:, :].opt()]))
            done("ar", cc)
            g.wait_ge(_cc, cc.n)
            qC.inc(g.dma_start(out=row12[:, :], in_=ar_out[:, 0:2]), 16)
            done("scales_in", qC)
        blk.gpsimd(_)

        def _(v):
            wait(v, "scales_in")
            cV.inc(v.tensor_scalar_max(row12[:, :], row12[:, :], EPS_G))
            v.drain()
            cV.inc(v.reciprocal(row12[:, :], row12[:, :]))
            v.drain()
            cV.inc(v.tensor_scalar(row12[:, :], row12[:, :], GAMMA, None, ALU.mult))
            vmark(v, "scales")
        blk.vector(_)

        def _(t_):
            wait(t_, "scales")
            cT.inc(t_.matmul(zhB[0:E, 0:2], ones8f[:, :], row12[:, :],
                             start=True, stop=True))
            done("rHdmm", cT)
        blk.tensor(_)

        def _(v):
            wait(v, "rHdmm")
            cV.inc(v.tensor_copy(rHd[:, :], zhB[0:E, 0:2]))
            v.drain()
            cV.inc(v.tensor_scalar(ltb[:, :], dT_sb[:, :], rHd[:, 1:2],
                                   None, ALU.mult))
            v.drain()
            cV.inc(v.scalar_tensor_tensor(ltb[:, :], H8T_sb[:, 1:TPC:2],
                                          rHd[:, 0:1], ltb[:, :],
                                          ALU.mult, ALU.add))
            vmark(v, "ltb")
            done("rHd_done", cV)
        blk.vector(_)

        for a in range(NA):
            emit_tgroup([ltb[:, a * 128:(a + 1) * 128]],
                        garb[:, a * 8:(a + 1) * 8], "v",
                        f"gl{a}", pre_waits=("ltb",))

        def _(v):
            for a in range(NA):
                wait(v, f"cp_gl{a}")
            for a in range(NA):
                gl = garb[:, a * 8:(a + 1) * 8]
                cV.inc(v.reduce_max(garb[:, 64 + a:65 + a], gl, axis=AX.X,
                                    negate=True))
            vmark(v, "negm")
        blk.vector(_)

        def _(s):
            wait(s, "negm")
            for a in range(NA):
                cS.inc(s.activation(garb[:, 32 + a * 8:40 + a * 8],
                                    garb[:, a * 8:(a + 1) * 8], AF.Exp,
                                    bias=garb[:, 64 + a:65 + a]))
            smark(s, "gexp")
        blk.scalar(_)

        def _(v):
            wait(v, "gexp")
            m1pad = garb[:, 72:80]
            cV.inc(v.memset(garb[:, 73:80], -5.0))
            v.drain()
            for a in range(NA):
                ge = garb[:, 32 + a * 8:40 + a * 8]
                gp = garb[:, a * 8:(a + 1) * 8]
                ssum = garb[:, 68 + a:69 + a]
                th = garb[:, 80 + a:81 + a]
                cV.inc(v.reduce_sum(ssum, ge, axis=AX.X))
                v.drain()
                cV.inc(v.reciprocal(ssum, ssum))
                v.drain()
                cV.inc(v.tensor_scalar(gp, ge, ssum, None, ALU.mult))
                v.drain()
                cV.inc(v.reduce_max(garb[:, 72:73], gp, axis=AX.X))
                v.drain()
                cV.inc(v.match_replace(ge, m1pad, gp, -1e30))
                v.drain()
                cV.inc(v.reduce_max(th, ge, axis=AX.X))
                v.drain()
                cV.inc(v.tensor_scalar(th, th, -2.0, None, ALU.mult))
                v.drain()
            vmark(v, "gprobs")
        blk.vector(_)

        def _(s):
            wait(s, "gprobs")
            for a in range(NA):
                cS.inc(s.activation(garb[:, 32 + a * 8:40 + a * 8],
                                    garb[:, a * 8:(a + 1) * 8], AF.Sigmoid,
                                    bias=garb[:, 80 + a:81 + a], scale=2.0))
            smark(s, "gsig")
        blk.scalar(_)

        def _(v):
            wait(v, "gsig")
            for a in range(NA):
                gp = garb[:, a * 8:(a + 1) * 8]
                gs = garb[:, 32 + a * 8:40 + a * 8]
                s2 = garb[:, 68 + a:69 + a]
                cV.inc(v.tensor_tensor(gp, gp, gs, ALU.mult))
                v.drain()
                cV.inc(v.reduce_sum(s2, gp, axis=AX.X))
                v.drain()
                cV.inc(v.tensor_scalar(s2, s2, 1e-9, None, ALU.add))
                v.drain()
                cV.inc(v.reciprocal(s2, s2))
                v.drain()
                cV.inc(v.tensor_scalar(gbf[:, a * 8:(a + 1) * 8], gp, s2,
                                       None, ALU.mult))
                v.drain()
            vmark(v, "gwn")
        blk.vector(_)

        for a in range(NA):
            emit_tgroup([gbf[:, a * 8:(a + 1) * 8]], wnT_sb[:, :], "v",
                        f"wnT{a}", pre_waits=("gwn",))

            def _(v, a=a):
                wait(v, f"cp_wnT{a}")
                cV.inc(v.tensor_copy(AP(wfT, a * 256, [[TPC, E], [2, 128]]),
                                     wnT_sb[:, :]))
                cV.inc(v.tensor_copy(AP(wfT, a * 256 + 1, [[TPC, E], [2, 128]]),
                                     wnT_sb[:, :]))
                vmark(v, f"wfT{a}")
            blk.vector(_)

        def _(t_):
            for a in range(NA):
                wait(t_, f"wfT{a}")
            wait(t_, "rHd_done")
            for i, off in enumerate(range(0, TPC, 512)):
                wdt = min(512, TPC - off)
                cT.inc(t_.matmul(zh[i % 2][0:R, 0:wdt], LiMEsb[:, :],
                                 wfT[:, off:off + wdt], start=True, stop=True))
            done("pmix", cT)
        blk.tensor(_)

        def _(v):
            wait(v, "pmix")
            for i, off in enumerate(range(0, TPC, 512)):
                wdt = min(512, TPC - off)
                cV.inc(v.tensor_copy(pm_sb[:, off:off + wdt],
                                     zh[i % 2][0:R, 0:wdt]))
            v.drain()
            cV.inc(v.tensor_tensor(zpT[0:R, :], zT_sb[:, :], pm_sb[:, :], ALU.mult))
            vmark(v, "zpT")
        blk.vector(_)



        MTo_v = (xf[:, :, :].rearrange("p a d -> p (a d)").bitcast(BF16)
                 .rearrange("p (c n) -> p c n", n=NSL)[:, 0:KC, :])
        mtbuf = [MTs[:, :, :], MTo_v]
        for r in range(NCORES):
            def _(sync, r=r):
                wait(sync, "ag1")
                if r == 1:
                    wait(sync, f"xnorm{NT-1}")
                if r >= 2:
                    wait(sync, f"mm_r{r-2}_t{NT-1}")
                qM[r % 2].inc(sync.dma_start(out=mtbuf[r % 2][:, 0:KC2, :],
                                             in_=ag_out1[r, :, :, :]), 16)
                done(f"mtload{r}h1", qM[r % 2])
                wait(sync, "ag2")
                qM2[r % 2].inc(sync.dma_start(out=mtbuf[r % 2][:, KC2:KC, :],
                                              in_=ag_out2[r, :, :, :]), 16)
                done(f"mtload{r}h2", qM2[r % 2])
            blk.sync(_)

            for t in range(NT):
                it = r * NT + t
                DEF = 2
                NIT = NCORES * NT

                def _(t_, r=r, t=t, it=it, DEF=DEF, NIT=NIT):
                    wait(t_, f"mtload{r}h1")
                    if it >= 4:
                        itp = it - 4
                        wait(t_, f"ev_r{itp // NT}_t{itp % NT}")
                    if r == 0 and t == 0:
                        wait(t_, "W8b_Bext"); wait(t_, "Bext_row")
                    if it >= DEF:
                        itd = it - DEF
                        rd_, td_ = itd // NT, itd % NT
                        if itd == 0:
                            wait(t_, "zpT")
                        cT.inc(t_.matmul(pp[itd % 4][:, 0:NSL],
                                         zpT[:, td_ * 128:(td_ + 1) * 128],
                                         Bextb[:, rd_ * NSL:(rd_ + 1) * NSL],
                                         start=False, stop=True))
                        done(f"mm_r{rd_}_t{td_}", cT)
                    ppt = pp[it % 4]
                    for kc in range(KC2):
                        t_.matmul(ppt[:, 0:NSL], xnT[:, kc, t * 128:(t + 1) * 128],
                                  mtbuf[r % 2][:, kc, :], start=(kc == 0), stop=False)
                    wait(t_, f"mtload{r}h2")
                    for kc in range(KC2, KC):
                        ins = t_.matmul(ppt[:, 0:NSL], xnT[:, kc, t * 128:(t + 1) * 128],
                                        mtbuf[r % 2][:, kc, :], start=False, stop=False)
                    cT.inc(ins)
                    done(f"mmp_r{r}_t{t}", cT)
                    if it == NIT - 1:
                        for itd in range(NIT - DEF, NIT):
                            rd_, td_ = itd // NT, itd % NT
                            if itd == 0:
                                wait(t_, "zpT")
                            cT.inc(t_.matmul(pp[itd % 4][:, 0:NSL],
                                             zpT[:, td_ * 128:(td_ + 1) * 128],
                                             Bextb[:, rd_ * NSL:(rd_ + 1) * NSL],
                                             start=False, stop=True))
                            done(f"mm_r{rd_}_t{td_}", cT)
                blk.tensor(_)

                ready = [it - DEF] if it >= DEF else []
                for itd in ready:
                    rd_, td_ = itd // NT, itd % NT

                    def _(v, rd_=rd_, td_=td_, itd=itd):
                        wait(v, f"mm_r{rd_}_t{td_}")
                        wait(v, f"xr_{rd_}_{td_}")
                        if itd >= 3:
                            itp = itd - 3
                            wait(v, f"st_r{itp // NT}_t{itp % NT}")
                        cV.inc(v.tensor_tensor(stg[:, itd % 3, :],
                                               pp[itd % 4][:, 0:NSL],
                                               xr[:, itd % 2, :], ALU.add))
                        vmark(v, f"ev_r{rd_}_t{td_}")
                    blk.vector(_)

                    def _(g, rd_=rd_, td_=td_, itd=itd):
                        wait(g, f"ev_r{rd_}_t{td_}")
                        sx = sS[itd % 3]
                        sx.inc(g.dma_start(
                            out=out3[:, td_, rd_ * NSL:(rd_ + 1) * NSL],
                            in_=stg[:, itd % 3, :]), 16)
                        done(f"st_r{rd_}_t{td_}", sx)
                    blk.gpsimd(_)

                def _(sync, r=r, t=t, it=it):
                    if it >= 2:
                        itp = it - 2
                        wait(sync, f"ev_r{itp // NT}_t{itp % NT}")
                    qR[it % 2].inc(sync.dma_start(out=xr[:, it % 2, :],
                                   in_=x3[:, t, r * NSL:(r + 1) * NSL]), 16)
                    done(f"xr_{r}_{t}", qR[it % 2])
                blk.sync(_)

        NIT = NCORES * NT
        for itd in range(NIT - 2, NIT):
            rd_, td_ = itd // NT, itd % NT

            def _(v, rd_=rd_, td_=td_, itd=itd):
                wait(v, f"mm_r{rd_}_t{td_}")
                wait(v, f"xr_{rd_}_{td_}")
                if itd >= 3:
                    itp = itd - 3
                    wait(v, f"st_r{itp // NT}_t{itp % NT}")
                cV.inc(v.tensor_tensor(stg[:, itd % 3, :],
                                       pp[itd % 4][:, 0:NSL],
                                       xr[:, itd % 2, :], ALU.add))
                vmark(v, f"ev_r{rd_}_t{td_}")
            blk.vector(_)

            def _(g, rd_=rd_, td_=td_, itd=itd):
                wait(g, f"ev_r{rd_}_t{td_}")
                sx = sS[itd % 3]
                sx.inc(g.dma_start(
                    out=out3[:, td_, rd_ * NSL:(rd_ + 1) * NSL],
                    in_=stg[:, itd % 3, :]), 16)
                done(f"st_r{rd_}_t{td_}", sx)
            blk.gpsimd(_)

        def _(g):
            for scnt in sS:
                g.wait_ge(scnt.sem, scnt.n)
        blk.gpsimd(_)

    return nc


def shard_inputs(data, mask, weight, bias, W, b, A, Bmat, LiMEs, TPC, D):
    xflat = np.ascontiguousarray(np.asarray(data, np.float32).reshape(-1, D))
    mask = np.asarray(mask, np.int32)
    weight = np.asarray(weight, np.float32)
    W = np.asarray(W, np.float32)
    NSL = D // NCORES
    identb = np.eye(128, dtype=ml_dtypes.bfloat16)
    identf = np.eye(128, dtype=np.float32)
    E32 = np.zeros((128, 4), dtype=ml_dtypes.bfloat16)
    for n in range(128):
        E32[n, n // 32] = 1
    Emat = np.ascontiguousarray(E32.T)
    in_maps = []
    for c in range(NCORES):
        in_maps.append({
            "x": np.ascontiguousarray(xflat[c * TPC:(c + 1) * TPC]),
            "mask_sl": np.ascontiguousarray(mask[c * NSL:(c + 1) * NSL]),
            "w_sl": np.ascontiguousarray(weight[c * NSL:(c + 1) * NSL]),
            "Wf_sl": np.ascontiguousarray(W[c * NSL:(c + 1) * NSL]),
            "W8": np.ascontiguousarray(W[0:E]),
            "biasrow": np.asarray(bias, np.float32).reshape(1, D),
            "brow": np.asarray(b, np.float32).reshape(1, D),
            "A": np.asarray(A, np.float32),
            "Bmat": np.asarray(Bmat, np.float32),
            "LiMEs": np.asarray(LiMEs, np.float32),
            "identb": identb,
            "identf": identf,
            "E32": E32,
            "Emat": Emat,
        })
    return in_maps



from concourse.bass_utils import run_bass_kernel_spmd

TPC_REAL = 1024
D_REAL = 4096


def kernel(L0_attn_data, L0_attn_mask, L0_attn_weight, L0_attn_bias,
           L0_ffn_W, L0_ffn_b, L0_ffn_A, L0_ffn_Bmat, L0_ffn_LiMEs):
    TPC, D = TPC_REAL, D_REAL
    nc = build_program(TPC, D, detect_races=True)
    in_maps = shard_inputs(L0_attn_data, L0_attn_mask, L0_attn_weight,
                           L0_attn_bias, L0_ffn_W, L0_ffn_b, L0_ffn_A,
                           L0_ffn_Bmat, L0_ffn_LiMEs, TPC, D)
    res = run_bass_kernel_spmd(nc, in_maps, core_ids=list(range(NCORES)))
    out = np.concatenate([np.asarray(r["out"]) for r in res.results], axis=0)
    return out.reshape(4, 2048, D).astype(np.float32, copy=False)


# version 15
# speedup vs baseline: 1.1742x; 1.1147x over previous
import numpy as np
import ml_dtypes
import concourse.bass as bass
import concourse.mybir as mybir
from concourse.bass_types import AP

F32 = mybir.dt.float32
BF16 = mybir.dt.bfloat16
I32 = mybir.dt.int32
AF = mybir.ActivationFunctionType
ALU = mybir.AluOpType
AX = mybir.AxisListType

NCORES = 8
R = 32
E = 8
BW = 32
EPS_LN = 1e-5
EPS_G = 1e-6
GAMMA = 0.5
SCALING = 0.5
ITD = 6


class Ctr:
    def __init__(self, sem):
        self.sem = sem
        self.n = 0

    def inc(self, instr, k=1):
        instr.then_inc(self.sem, k)
        self.n += k
        return self.n


def build_program(TPC, D, detect_races=True):
    NT = TPC // 128
    KC = D // 128
    NSL = D // NCORES
    NHF = NSL // 2
    KQ = D // 4
    KCQ = KC // 4
    NTL = NSL // 128
    AN = TPC // 2
    NA = AN // 128
    NH = 4 * NTL
    KB = D // BW
    NIT = 2 * 4 * NT

    nc = bass.Bass("TRN2", target_bir_lowering=False,
                   detect_race_conditions=detect_races)

    x_in = nc.dram_tensor("x", [TPC, D], F32, kind="ExternalInput")
    mask_sl = nc.dram_tensor("mask_sl", [NSL, D], I32, kind="ExternalInput")
    w_sl = nc.dram_tensor("w_sl", [NSL, D], F32, kind="ExternalInput")
    Wf_sl = nc.dram_tensor("Wf_sl", [NSL, D], F32, kind="ExternalInput")
    W8_in = nc.dram_tensor("W8", [E, D], F32, kind="ExternalInput")
    biasrow_in = nc.dram_tensor("biasrow", [1, D], F32, kind="ExternalInput")
    brow_in = nc.dram_tensor("brow", [1, D], F32, kind="ExternalInput")
    A_in = nc.dram_tensor("A", [D, R], F32, kind="ExternalInput")
    Bmat_in = nc.dram_tensor("Bmat", [R, D], F32, kind="ExternalInput")
    LiMEs_in = nc.dram_tensor("LiMEs", [E, R], F32, kind="ExternalInput")
    identb_in = nc.dram_tensor("identb", [128, 128], BF16, kind="ExternalInput")
    identf_in = nc.dram_tensor("identf", [128, 128], F32, kind="ExternalInput")
    E32_in = nc.dram_tensor("E32", [128, 4], BF16, kind="ExternalInput")
    Emat_in = nc.dram_tensor("Emat", [4, 128], BF16, kind="ExternalInput")
    out_d = nc.dram_tensor("out", [TPC, D], F32, kind="ExternalOutput")

    ag_in1 = nc.dram_tensor("ag_in1", [128, KC, NHF], BF16)
    ag_in2 = nc.dram_tensor("ag_in2", [128, KC, NHF], BF16)
    ag_out1 = nc.dram_tensor("ag_out1", [NCORES, 128, KC, NHF], BF16,
                             addr_space="Shared")
    ag_out2 = nc.dram_tensor("ag_out2", [NCORES, 128, KC, NHF], BF16,
                             addr_space="Shared")
    ar_in = nc.dram_tensor("ar_in", [1, 8], F32)
    ar_out = nc.dram_tensor("ar_out", [1, 8], F32)
    RG = [list(range(NCORES))]

    x3 = x_in[:, :].rearrange("(t p) d -> p t d", p=128)

    from contextlib import ExitStack
    ES = ExitStack()
    if True:
        xnT = ES.enter_context(nc.sbuf_tensor("xnT", [128, KC, TPC], BF16))
        AW8b = ES.enter_context(nc.sbuf_tensor("AW8b", [128, KC, R + E], BF16))
        zT_sb = ES.enter_context(nc.sbuf_tensor("zT_sb", [R, TPC], BF16))
        H8T_sb = ES.enter_context(nc.sbuf_tensor("H8T_sb", [E, TPC], BF16))
        dT_sb = ES.enter_context(nc.sbuf_tensor("dT_sb", [E, AN], BF16))
        ltb = ES.enter_context(nc.sbuf_tensor("ltb", [E, AN], BF16))
        wfT = ES.enter_context(nc.sbuf_tensor("wfT", [E, TPC], BF16))
        pm_sb = ES.enter_context(nc.sbuf_tensor("pm_sb", [R, TPC], BF16))
        zpT = ES.enter_context(nc.sbuf_tensor("zpT", [R + 1, TPC], BF16))
        b8ext = ES.enter_context(nc.sbuf_tensor("b8ext", [1, R + E], BF16))
        ones128b = ES.enter_context(nc.sbuf_tensor("ones128b", [1, 128], BF16))
        ones8f = ES.enter_context(nc.sbuf_tensor("ones8f", [1, E], F32))
        LiMEsb = ES.enter_context(nc.sbuf_tensor("LiMEsb", [E, R], BF16))
        LT_sb = ES.enter_context(nc.sbuf_tensor("LT_sb", [R, E], BF16))
        identb = ES.enter_context(nc.sbuf_tensor("identb_s", [128, 128], BF16))
        identf = ES.enter_context(nc.sbuf_tensor("identf_s", [128, 128], F32))
        E32b = ES.enter_context(nc.sbuf_tensor("E32b_s", [128, 4], BF16))
        Ematb = ES.enter_context(nc.sbuf_tensor("Ematb", [4, 128], BF16))
        st = ES.enter_context(nc.sbuf_tensor("stats", [128, 8], F32))
        epsc = ES.enter_context(nc.sbuf_tensor("epsc", [128, 1], F32))
        mx2 = ES.enter_context(nc.sbuf_tensor("mx2", [E, 2], F32))
        m21 = ES.enter_context(nc.sbuf_tensor("m21", [2, 1], F32))
        row12 = ES.enter_context(nc.sbuf_tensor("row12", [1, 2], F32))
        arst = ES.enter_context(nc.sbuf_tensor("arst", [1, 8], F32))
        rHd = ES.enter_context(nc.sbuf_tensor("rHd", [E, 2], F32))
        garb = ES.enter_context(nc.sbuf_tensor("garb", [128, 84], F32))
        gbf = ES.enter_context(nc.sbuf_tensor("gbf", [128, 32], BF16))
        wnT_sb = ES.enter_context(nc.sbuf_tensor("wnT_sb", [E, 128], BF16))
        W8b = ES.enter_context(nc.sbuf_tensor("W8b", [E, D], BF16))
        red1w = ES.enter_context(nc.sbuf_tensor("red1w", [128, KB], BF16))
        bm4T = ES.enter_context(nc.sbuf_tensor("bm4T", [4, KB], BF16))
        bmex_all = ES.enter_context(nc.sbuf_tensor("bmex_all", [128, NTL, KB], F32))
        Mtb = ES.enter_context(nc.sbuf_tensor("Mtb", [128, KQ], BF16))
        xnb = ES.enter_context(nc.sbuf_tensor("xnb", [128, D], BF16))
        MTs = ES.enter_context(nc.sbuf_tensor("MTs", [128, 2, KC, NHF], BF16))
        xf = ES.enter_context(nc.sbuf_tensor("xf", [128, 2, D], F32))
        SC = ES.enter_context(nc.sbuf_tensor("SC", [128, 6144], F32))
        stg = ES.enter_context(nc.sbuf_tensor("stg", [128, 3, NSL], F32))
        ppA = ES.enter_context(nc.psum_tensor("ppA", [128, 512], F32))
        ppB = ES.enter_context(nc.psum_tensor("ppB", [128, 512], F32))
        zhA = ES.enter_context(nc.psum_tensor("zhA", [R + E, 512], F32))
        zhB = ES.enter_context(nc.psum_tensor("zhB", [R + E, 512], F32))
        ptA = ES.enter_context(nc.psum_tensor("ptA", [128, 1024], BF16))
        ptB = ES.enter_context(nc.psum_tensor("ptB", [128, 1024], BF16))
        pfA = ES.enter_context(nc.psum_tensor("pfA", [128, 512], F32))
        pfB = ES.enter_context(nc.psum_tensor("pfB", [128, 512], F32))
        _qG = ES.enter_context(nc.semaphore("qG"))
        _qB = ES.enter_context(nc.semaphore("qB"))
        _qC = ES.enter_context(nc.semaphore("qC"))
        _qD = ES.enter_context(nc.semaphore("qD"))
        _cT = ES.enter_context(nc.semaphore("cT"))
        _cV = ES.enter_context(nc.semaphore("cV"))
        _cS = ES.enter_context(nc.semaphore("cS"))
        _cG = ES.enter_context(nc.semaphore("cG"))
        _cc = ES.enter_context(nc.semaphore("cc"))
        _qX0 = ES.enter_context(nc.semaphore("qX0"))
        _qX1 = ES.enter_context(nc.semaphore("qX1"))
        _qM0 = ES.enter_context(nc.semaphore("qM0"))
        _qM1 = ES.enter_context(nc.semaphore("qM1"))
        _qR0 = ES.enter_context(nc.semaphore("qR0"))
        _qR1 = ES.enter_context(nc.semaphore("qR1"))
        _qW0 = ES.enter_context(nc.semaphore("qW0"))
        _qW1 = ES.enter_context(nc.semaphore("qW1"))
        _qE0 = ES.enter_context(nc.semaphore("qE0"))
        _qE1 = ES.enter_context(nc.semaphore("qE1"))
        _qF = ES.enter_context(nc.semaphore("qF"))
        _s0 = ES.enter_context(nc.semaphore("s0"))
        _s1 = ES.enter_context(nc.semaphore("s1"))
        _s2 = ES.enter_context(nc.semaphore("s2"))
        blk = ES.enter_context(nc.Block())
    with ES:
        qG, qB = Ctr(_qG), Ctr(_qB)
        qC, qD = Ctr(_qC), Ctr(_qD)
        cT, cV, cS, cG, cc = Ctr(_cT), Ctr(_cV), Ctr(_cS), Ctr(_cG), Ctr(_cc)
        qX = [Ctr(_qX0), Ctr(_qX1)]
        qM = [Ctr(_qM0), Ctr(_qM1)]
        qR = [Ctr(_qR0), Ctr(_qR1)]
        qW = [Ctr(_qW0), Ctr(_qW1)]
        qE = [Ctr(_qE0), Ctr(_qE1)]
        qF = Ctr(_qF)
        sS = [Ctr(_s0), Ctr(_s1), Ctr(_s2)]
        pp = [ppA, ppB, pfA, pfB]
        zh = [zhA, zhB]
        pt = [ptA, ptB]
        ev = {}

        wbv = SC[:, 0:2048].rearrange("p (a b) -> p a b", a=2)
        maskbv = SC[:, 2048:4096].bitcast(I32).rearrange("p (a b) -> p a b", a=2)
        Wbbv = SC[:, 4096:5120].bitcast(BF16).rearrange("p (a b) -> p a b", a=2)
        stg2 = SC[:, 0:ITD * NSL].rearrange("p (a b) -> p a b", a=ITD)
        xrv = SC[:, ITD * NSL:ITD * NSL + 2 * NSL].rearrange("p (a b) -> p a b", a=2)
        Bx2 = SC[:, 4096:6144].bitcast(BF16)
        mt2s = [
            MTs[:, :, :, :].rearrange("p h c n -> p (h c n)")
               .rearrange("p (c s n) -> p c s n", c=KC, s=2),
            xf[:, :, :].rearrange("p a d -> p (a d)").bitcast(BF16)
               .rearrange("p (c s n) -> p c s n", c=KC, s=2),
        ]

        def done(name, ctr):
            ev[name] = (ctr, ctr.n)

        def wait(eng, name):
            c, v = ev[name]
            if v > 0:
                eng.wait_ge(c.sem, v)

        def vmark(v, name):
            cV.inc(v.drain())
            done(name, cV)

        def smark(s, name):
            cS.inc(s.drain())
            done(name, cS)

        tpar = [0]

        def emit_tgroup(srcs, dst_ap, copy_engine, name, pre_waits=()):
            par = tpar[0] % 2
            tpar[0] += 1
            prev = ev.get(f"ptfree{par}")
            w_out = srcs[0].shape[1]

            def _t(t):
                for wn_ in pre_waits:
                    wait(t, wn_)
                if prev is not None and prev[1] > 0:
                    t.wait_ge(prev[0].sem, prev[1])
                off = 0
                for i, sap in enumerate(srcs):
                    p_in = sap.shape[0]
                    ins = t.transpose(pt[par][0:w_out, off:off + p_in], sap,
                                      identb[0:p_in, 0:p_in])
                    if i == len(srcs) - 1:
                        cT.inc(ins)
                    off += p_in
                done(f"tp_{name}", cT)
            blk.tensor(_t)

            def _c(c):
                wait(c, f"tp_{name}")
                total = sum(sap.shape[0] for sap in srcs)
                srcp = pt[par][0:w_out, 0:total]
                if dst_ap.ndim == 3:
                    srcp = srcp.rearrange("p (a b) -> p a b", b=dst_ap.shape[2])
                if copy_engine == "v":
                    cV.inc(c.tensor_copy(dst_ap, srcp))
                    done(f"ptfree{par}", cV)
                    done(f"cp_{name}", cV)
                else:
                    cS.inc(c.copy(dst_ap, srcp))
                    done(f"ptfree{par}", cS)
                    done(f"cp_{name}", cS)
            (blk.vector if copy_engine == "v" else blk.scalar)(_c)

        def _(g):
            qG.inc(g.dma_start(out=AW8b[:, :, 0:R],
                               in_=A_in[:, :].rearrange("(c k) r -> k c r", k=128)), 16)
            qG.inc(g.dma_start(out=W8b[:, :], in_=W8_in[:, :]), 16)
            qG.inc(g.dma_start(out=LiMEsb[:, :], in_=LiMEs_in[:, :]), 16)
            qG.inc(g.dma_start(out=identb[:, :], in_=identb_in[:, :]), 16)
            qG.inc(g.dma_start(out=identf[:, :], in_=identf_in[:, :]), 16)
            qG.inc(g.dma_start(out=E32b[:, :], in_=E32_in[:, :]), 16)
            qG.inc(g.dma_start(out=Ematb[:, :], in_=Emat_in[:, :]), 16)
            for nm in ("smalls", "consts"):
                done(nm, qG)
            cG.inc(g.memset(ones128b[:, :], 1.0))
            cG.inc(g.memset(ones8f[:, :], 1.0))
            cG.inc(g.memset(b8ext[:, :], 0.0))
            cG.inc(g.memset(zpT[R:R + 1, :], 1.0))
            cG.inc(g.memset(epsc[:, :], EPS_LN))
            done("memsets", cG)
            g.wait_ge(_cG, cG.n)
            qF.inc(g.dma_start(out=b8ext[0:1, R:R + E], in_=brow_in[0:1, 0:E],
                               allow_slow_non_contiguous=True), 16)
            done("b8l", qF)
        blk.gpsimd(_)

        ev["sconsts_v"] = ev["b8l"]

        for g8 in range(0, KC, 8):
            kcn = min(8, KC - g8)
            srcs = [W8b[:, kc * 128:(kc + 1) * 128] for kc in range(g8, g8 + kcn)]
            emit_tgroup(srcs, AW8b[:, g8:g8 + kcn, R:R + E], "v", f"w8g{g8}",
                        pre_waits=("smalls", "consts"))
        emit_tgroup([LiMEsb[:, :]], LT_sb[:, :], "v", "limt",
                    pre_waits=("sconsts_v",))

        def emit_mask_eighths(nt):
            KBQ4 = KQ // BW

            def emit_mload(nt, e):
                def _(g, nt=nt, e=e):
                    m = nt * 4 + e
                    if m >= 2:
                        pm = m - 2
                        wait(g, f"red1_{pm // 4}_{pm % 4}")
                    if nt > 0 and e < 2:
                        wait(g, f"bsum{nt-1}")
                    ks = e * KQ
                    qE[e % 2].inc(g.dma_start(out=maskbv[:, e % 2, :],
                                  in_=mask_sl[nt * 128:nt * 128 + 128,
                                              ks:ks + KQ]), 16)
                    done(f"mload{nt}_{e}", qE[e % 2])
                blk.gpsimd(_)

            def emit_mred(nt, e):
                def _(v, nt=nt, e=e):
                    wait(v, f"mload{nt}_{e}")
                    if nt > 0 and e == 0:
                        wait(v, f"bsum{nt-1}")
                    with nc.allow_low_precision(reason="mask block sums <=32 exact in bf16"):
                        cV.inc(v.reduce_sum(
                            red1w[:, e * KBQ4:(e + 1) * KBQ4],
                            maskbv[:, e % 2, :].rearrange("p (g j) -> p g j", j=BW),
                            axis=AX.X))
                    vmark(v, f"red1_{nt}_{e}")
                blk.vector(_)

            emit_mload(nt, 0)
            emit_mload(nt, 1)
            for e in range(4):
                emit_mred(nt, e)
                if e + 2 < 4:
                    emit_mload(nt, e + 2)

        def emit_bm_chain(nt):
            def _(t_, nt=nt):
                for e in range(4):
                    wait(t_, f"red1_{nt}_{e}")
                if nt > 0:
                    wait(t_, f"bm01_{nt-1}")
                cT.inc(t_.matmul(pfA[0:4, 0:KB], E32b[:, :], red1w[:, 0:KB],
                                 start=True, stop=True))
                done(f"bsum{nt}", cT)
            blk.tensor(_)

            def _(v, nt=nt):
                wait(v, f"bsum{nt}")
                if nt > 0:
                    wait(v, f"bexp{nt-1}")
                cV.inc(v.tensor_single_scalar(bm4T[:, 0:KB], pfA[0:4, 0:KB],
                                              0.0, ALU.is_gt))
                vmark(v, f"bm01_{nt}")
            blk.vector(_)

            def _(t_, nt=nt):
                wait(t_, f"bm01_{nt}")
                if nt > 0:
                    wait(t_, f"bmex_{nt-1}")
                cT.inc(t_.matmul(pfB[:, 0:KB], Ematb[:, :], bm4T[:, 0:KB],
                                 start=True, stop=True))
                done(f"bexp{nt}", cT)
            blk.tensor(_)

            def _(v, nt=nt):
                wait(v, f"bexp{nt}")
                cV.inc(v.tensor_copy(bmex_all[:, nt, :], pfB[:, 0:KB]))
                vmark(v, f"bmex_{nt}")
            blk.vector(_)

        def emit_quarter(h):
            nt, kh = h // 4, h % 4
            rs, ks = nt * 128, kh * KQ
            kbq = KQ // BW

            def _(g, h=h, rs=rs, ks=ks):
                if h > 1:
                    wait(g, f"mtb{h-2}")
                qW[h % 2].inc(g.dma_start(out=wbv[:, h % 2, :],
                              in_=w_sl[rs:rs + 128, ks:ks + KQ]), 16)
                qW[h % 2].inc(g.dma_start(out=Wbbv[:, h % 2, :],
                              in_=Wf_sl[rs:rs + 128, ks:ks + KQ]), 16)
                done(f"wload{h}", qW[h % 2])
            blk.gpsimd(_)

            def _(v, h=h, kh=kh, nt=nt, kbq=kbq):
                wait(v, f"wload{h}")
                wait(v, f"bmex_{nt}")
                if h > 0:
                    wait(v, f"mtb_done{h-1}")
                w3i = AP(SC, (h % 2) * KQ, [[6144, 128], [BW, kbq], [1, BW]])
                w3o = AP(Mtb, 0, [[KQ, 128], [BW, kbq], [1, BW]])
                b3 = AP(bmex_all, nt * KB + kh * kbq,
                        [[NTL * KB, 128], [1, kbq], [0, BW]])
                cV.inc(v.tensor_tensor(w3o, w3i, b3, ALU.mult))
                v.drain()
                cV.inc(v.tensor_tensor(Mtb[:, :], Mtb[:, :], Wbbv[:, h % 2, :],
                                       ALU.add))
                vmark(v, f"mtb{h}")
            blk.vector(_)

            srcs = [Mtb[:, kc2 * 128:(kc2 + 1) * 128] for kc2 in range(KCQ)]
            emit_tgroup(srcs,
                        MTs[:, nt // 2, kh * KCQ:(kh + 1) * KCQ,
                            (nt % 2) * 128:(nt % 2) * 128 + 128],
                        "v", f"mtg{h}", pre_waits=(f"mtb{h}",))
            ev[f"mtb_done{h}"] = ev[f"tp_mtg{h}"]
            ev[f"q_done{h}"] = ev[f"cp_mtg{h}"]

        for nt in range(NTL):
            emit_mask_eighths(nt)
            emit_bm_chain(nt)
            for kh in range(4):
                emit_quarter(nt * 4 + kh)
            if nt == 1:
                def _(g):
                    for h in range(8):
                        wait(g, f"q_done{h}")
                    qD.inc(g.dma_start(out=ag_in1[:, :, :],
                                       in_=MTs[:, 0, :, :]), 16)
                    g.wait_ge(_qD, qD.n)
                    cc.inc(g.collective_compute(
                        "AllGather", ALU.bypass, replica_groups=RG,
                        ins=[ag_in1[:, :, :].opt()],
                        outs=[ag_out1[:, :, :, :].opt()]))
                    done("ag1", cc)
                blk.gpsimd(_)

        def _(g):
            for h in range(8, NH):
                wait(g, f"q_done{h}")
            qD.inc(g.dma_start(out=ag_in2[:, :, :], in_=MTs[:, 1, :, :]), 16)
            done("ag_in2", qD)
        blk.gpsimd(_)

        Bm16 = Bmat_in[:, :].rearrange("r (a c) -> r a c", c=NHF)
        br16 = brow_in[:, :].rearrange("r (a c) -> r a c", c=NHF)
        bi16 = biasrow_in[:, :].rearrange("r (a c) -> r a c", c=NHF)

        def _(g):
            wait(g, f"mtb{NH-1}")
            for pd in range(2):
                qB.inc(g.dma_start(
                    out=Bx2[0:R, pd * 2048:(pd + 1) * 2048]
                    .rearrange("r (a c) -> r a c", c=NHF),
                    in_=Bm16[:, pd:16:2, :]), 16)
                qB.inc(g.dma_start(
                    out=Bx2[R:R + 1, pd * 2048:(pd + 1) * 2048]
                    .rearrange("r (a c) -> r a c", c=NHF),
                    in_=br16[:, pd:16:2, :]), 16)
            g.wait_ge(_qB, qB.n)
            for pd in range(2):
                qB.inc(g.dma_start(
                    out=Bx2[R:R + 1, pd * 2048:(pd + 1) * 2048]
                    .rearrange("r (a c) -> r a c", c=NHF),
                    in_=bi16[:, pd:16:2, :],
                    accum_op=ALU.add), 16)
            done("Bx2_l", qB)
        blk.gpsimd(_)

        for t in range(NT):
            def _(sync, t=t):
                if t >= 2:
                    wait(sync, f"xnorm{t-2}")
                qX[t % 2].inc(sync.dma_start(out=xf[:, t % 2, :], in_=x3[:, t, :]), 16)
                done(f"x{t}", qX[t % 2])
            blk.sync(_)

            xtile = xf[:, t % 2, :]

            def _(s, t=t, xtile=xtile):
                wait(s, f"x{t}")
                if t == 0:
                    wait(s, "memsets")
                if t >= 2:
                    wait(s, f"zh{t-2}")
                cS.inc(s.activation(xnT[:, :, t * 128:(t + 1) * 128], xtile,
                                    AF.Copy, accum_out=st[:, 0:1]))
                s.drain()
                cS.inc(s.activation(xnT[:, :, t * 128:(t + 1) * 128], xtile,
                                    AF.Square, accum_out=st[:, 1:2]))
                s.drain()
                if t >= 1:
                    wait(s, f"xnorm{t-1}")
                cS.inc(s.mul(st[:, 2:3], st[:, 0:1], 1.0 / D))
                cS.inc(s.mul(st[:, 3:4], st[:, 1:2], 1.0 / D))
                s.drain()
                cS.inc(s.mul(st[:, 7:8], st[:, 2:3], st[:, 2:3]))
                s.drain()
                cS.inc(s.mul(st[:, 7:8], st[:, 7:8], -1.0))
                s.drain()
                cS.inc(s.add(st[:, 4:5], st[:, 3:4], st[:, 7:8]))
                s.drain()
                cS.inc(s.activation(st[:, 7:8], st[:, 4:5], AF.Sqrt,
                                    bias=epsc[:, 0:1]))
                smark(s, f"stats{t}")
            blk.scalar(_)

            def _(v, t=t, xtile=xtile):
                wait(v, f"stats{t}")
                if t >= 1:
                    wait(v, f"xn_done{t-1}")
                v.drain()
                cV.inc(v.reciprocal(st[:, 5:6], st[:, 7:8]))
                v.drain()
                cV.inc(v.tensor_scalar(st[:, 6:7], st[:, 2:3], st[:, 5:6], -1.0,
                                       ALU.mult, ALU.mult))
                v.drain()
                cV.inc(v.tensor_scalar(xnb[:, :], xtile, st[:, 5:6], st[:, 6:7],
                                       ALU.mult, ALU.add))
                vmark(v, f"xnorm{t}")
            blk.vector(_)

            NGRP = (KC + 7) // 8
            for gi in range(NGRP):
                kc0 = gi * 8
                kcn = min(8, KC - kc0)
                srcs = [xnb[:, kc * 128:(kc + 1) * 128]
                        for kc in range(kc0, kc0 + kcn)]
                emit_tgroup(srcs, xnT[:, kc0:kc0 + kcn, t * 128:(t + 1) * 128],
                            "v", f"xng{t}_{gi}",
                            pre_waits=(f"xnorm{t}",))
            ev[f"xn_done{t}"] = ev[f"tp_xng{t}_{NGRP-1}"]

            def _(t_, t=t, NGRP=NGRP):
                for gi in range(NGRP):
                    wait(t_, f"cp_xng{t}_{gi}")
                if t == 0:
                    wait(t_, "sconsts_v")
                    wait(t_, f"cp_w8g{(KC - 1) // 8 * 8}")
                    wait(t_, "memsets")
                if t >= 2:
                    wait(t_, f"zhfree{t % 2}")
                ztile = zh[t % 2]
                for kc in range(KC):
                    t_.matmul(ztile[:, 0:128], AW8b[:, kc, :],
                              xnT[:, kc, t * 128:(t + 1) * 128],
                              start=(kc == 0), stop=False)
                cT.inc(t_.matmul(ztile[:, 0:128], b8ext[:, :], ones128b[:, :],
                                 start=False, stop=True))
                done(f"zh{t}", cT)
            blk.tensor(_)

            def _(v, t=t):
                wait(v, f"zh{t}")
                cV.inc(v.tensor_copy(zT_sb[:, t * 128:(t + 1) * 128],
                                     zh[t % 2][0:R, 0:128]))
                cV.inc(v.tensor_copy(H8T_sb[:, t * 128:(t + 1) * 128],
                                     zh[t % 2][R:R + E, 0:128]))
                vmark(v, f"zcp{t}")
                done(f"hcp{t}", cV)
                done(f"zhfree{t % 2}", cV)
            blk.vector(_)

        def _(v):
            for t in range(NT):
                wait(v, f"hcp{t}")
            cV.inc(v.reduce_max(mx2[:, 0:1], H8T_sb[:, 1:TPC:2], axis=AX.X,
                                apply_absolute_value=True))
            vmark(v, "hmax")
        blk.vector(_)

        def _(t_):
            for t in range(NT):
                wait(t_, f"zcp{t}")
            wait(t_, "cp_limt")
            wait(t_, "zhfree0"); wait(t_, "zhfree1")
            cT.inc(t_.matmul(zhA[0:E, 0:AN], LT_sb[:, :], zT_sb[:, 1:TPC:2],
                             start=True, stop=True))
            done("dTmm", cT)
        blk.tensor(_)

        def _(s):
            wait(s, "dTmm")
            cS.inc(s.copy(dT_sb[:, :], zhA[0:E, 0:AN]))
            smark(s, "dTcp")
        blk.scalar(_)

        def _(s):
            wait(s, "Bx2_l")
            cS.inc(s.mul(Bx2[0:R, :], Bx2[0:R, :], SCALING))
            smark(s, "Bx2")
        blk.scalar(_)

        def _(v):
            wait(v, "dTcp")
            cV.inc(v.reduce_max(mx2[:, 1:2], dT_sb[:, :], axis=AX.X,
                                apply_absolute_value=True))
            vmark(v, "dmax")
        blk.vector(_)

        def _(t_):
            wait(t_, "hmax"); wait(t_, "dmax")
            cT.inc(t_.transpose(zhB[0:2, 0:E], mx2[:, :], identf[0:E, 0:E]))
            done("mxT", cT)
        blk.tensor(_)

        def _(v):
            wait(v, "mxT")
            cV.inc(v.reduce_max(m21[:, :], zhB[0:2, 0:E], axis=AX.X))
            vmark(v, "m21")
        blk.vector(_)

        def _(t_):
            wait(t_, "m21")
            cT.inc(t_.transpose(zhA[0:1, 0:2], m21[:, :], identf[0:2, 0:2]))
            done("m21T", cT)
        blk.tensor(_)

        def _(v):
            wait(v, "m21T")
            cV.inc(v.memset(arst[:, :], 0.0))
            v.drain()
            cV.inc(v.tensor_copy(arst[0:1, 0:2], zhA[0:1, 0:2]))
            vmark(v, "arst")
        blk.vector(_)

        def _(g):
            wait(g, "arst")
            qC.inc(g.dma_start(out=ar_in[:, :], in_=arst[:, :]), 16)
            g.wait_ge(_qC, qC.n)
            cc.inc(g.collective_compute("AllReduce", ALU.max, replica_groups=RG,
                                        ins=[ar_in[:, :].opt()],
                                        outs=[ar_out[:, :].opt()]))
            done("ar", cc)
            g.wait_ge(_cc, cc.n)
            qC.inc(g.dma_start(out=row12[:, :], in_=ar_out[:, 0:2]), 16)
            done("scales_in", qC)
        blk.gpsimd(_)

        def _(g):
            wait(g, "ag_in2")
            cc.inc(g.collective_compute(
                "AllGather", ALU.bypass, replica_groups=RG,
                ins=[ag_in2[:, :, :].opt()],
                outs=[ag_out2[:, :, :, :].opt()]))
            done("ag2", cc)
        blk.gpsimd(_)

        def _(v):
            wait(v, "scales_in")
            cV.inc(v.tensor_scalar_max(row12[:, :], row12[:, :], EPS_G))
            v.drain()
            cV.inc(v.reciprocal(row12[:, :], row12[:, :]))
            v.drain()
            cV.inc(v.tensor_scalar(row12[:, :], row12[:, :], GAMMA, None, ALU.mult))
            vmark(v, "scales")
        blk.vector(_)

        def _(t_):
            wait(t_, "scales")
            cT.inc(t_.matmul(zhB[0:E, 0:2], ones8f[:, :], row12[:, :],
                             start=True, stop=True))
            done("rHdmm", cT)
        blk.tensor(_)

        def _(v):
            wait(v, "rHdmm")
            cV.inc(v.tensor_copy(rHd[:, :], zhB[0:E, 0:2]))
            v.drain()
            cV.inc(v.tensor_scalar(ltb[:, :], dT_sb[:, :], rHd[:, 1:2],
                                   None, ALU.mult))
            v.drain()
            cV.inc(v.scalar_tensor_tensor(ltb[:, :], H8T_sb[:, 1:TPC:2],
                                          rHd[:, 0:1], ltb[:, :],
                                          ALU.mult, ALU.add))
            vmark(v, "ltb")
            done("rHd_done", cV)
        blk.vector(_)

        for a in range(NA):
            emit_tgroup([ltb[:, a * 128:(a + 1) * 128]],
                        garb[:, a * 8:(a + 1) * 8], "v",
                        f"gl{a}", pre_waits=("ltb",))

        def _(v):
            for a in range(NA):
                wait(v, f"cp_gl{a}")
            for a in range(NA):
                gl = garb[:, a * 8:(a + 1) * 8]
                cV.inc(v.reduce_max(garb[:, 64 + a:65 + a], gl, axis=AX.X,
                                    negate=True))
            vmark(v, "negm")
        blk.vector(_)

        def _(s):
            wait(s, "negm")
            for a in range(NA):
                cS.inc(s.activation(garb[:, 32 + a * 8:40 + a * 8],
                                    garb[:, a * 8:(a + 1) * 8], AF.Exp,
                                    bias=garb[:, 64 + a:65 + a]))
            smark(s, "gexp")
        blk.scalar(_)

        def _(v):
            wait(v, "gexp")
            m1pad = garb[:, 72:80]
            cV.inc(v.memset(garb[:, 73:80], -5.0))
            v.drain()
            for a in range(NA):
                ge = garb[:, 32 + a * 8:40 + a * 8]
                gp = garb[:, a * 8:(a + 1) * 8]
                ssum = garb[:, 68 + a:69 + a]
                th = garb[:, 80 + a:81 + a]
                cV.inc(v.reduce_sum(ssum, ge, axis=AX.X))
                v.drain()
                cV.inc(v.reciprocal(ssum, ssum))
                v.drain()
                cV.inc(v.tensor_scalar(gp, ge, ssum, None, ALU.mult))
                v.drain()
                cV.inc(v.reduce_max(garb[:, 72:73], gp, axis=AX.X))
                v.drain()
                cV.inc(v.match_replace(ge, m1pad, gp, -1e30))
                v.drain()
                cV.inc(v.reduce_max(th, ge, axis=AX.X))
                v.drain()
                cV.inc(v.tensor_scalar(th, th, -2.0, None, ALU.mult))
                v.drain()
            vmark(v, "gprobs")
        blk.vector(_)

        def _(s):
            wait(s, "gprobs")
            for a in range(NA):
                cS.inc(s.activation(garb[:, 32 + a * 8:40 + a * 8],
                                    garb[:, a * 8:(a + 1) * 8], AF.Sigmoid,
                                    bias=garb[:, 80 + a:81 + a], scale=2.0))
            smark(s, "gsig")
        blk.scalar(_)

        def _(v):
            wait(v, "gsig")
            for a in range(NA):
                gp = garb[:, a * 8:(a + 1) * 8]
                gs = garb[:, 32 + a * 8:40 + a * 8]
                s2 = garb[:, 68 + a:69 + a]
                cV.inc(v.tensor_tensor(gp, gp, gs, ALU.mult))
                v.drain()
                cV.inc(v.reduce_sum(s2, gp, axis=AX.X))
                v.drain()
                cV.inc(v.tensor_scalar(s2, s2, 1e-9, None, ALU.add))
                v.drain()
                cV.inc(v.reciprocal(s2, s2))
                v.drain()
                cV.inc(v.tensor_scalar(gbf[:, a * 8:(a + 1) * 8], gp, s2,
                                       None, ALU.mult))
                v.drain()
            vmark(v, "gwn")
        blk.vector(_)

        for a in range(NA):
            emit_tgroup([gbf[:, a * 8:(a + 1) * 8]], wnT_sb[:, :], "v",
                        f"wnT{a}", pre_waits=("gwn",))

            def _(v, a=a):
                wait(v, f"cp_wnT{a}")
                cV.inc(v.tensor_copy(AP(wfT, a * 256, [[TPC, E], [2, 128]]),
                                     wnT_sb[:, :]))
                cV.inc(v.tensor_copy(AP(wfT, a * 256 + 1, [[TPC, E], [2, 128]]),
                                     wnT_sb[:, :]))
                vmark(v, f"wfT{a}")
            blk.vector(_)

        def _(t_):
            for a in range(NA):
                wait(t_, f"wfT{a}")
            wait(t_, "rHd_done")
            for i, off in enumerate(range(0, TPC, 512)):
                wdt = min(512, TPC - off)
                cT.inc(t_.matmul(zh[i % 2][0:R, 0:wdt], LiMEsb[:, :],
                                 wfT[:, off:off + wdt], start=True, stop=True))
            done("pmix", cT)
        blk.tensor(_)

        def _(v):
            wait(v, "pmix")
            for i, off in enumerate(range(0, TPC, 512)):
                wdt = min(512, TPC - off)
                cV.inc(v.tensor_copy(pm_sb[:, off:off + wdt],
                                     zh[i % 2][0:R, 0:wdt]))
            v.drain()
            cV.inc(v.tensor_tensor(zpT[0:R, :], zT_sb[:, :], pm_sb[:, :], ALU.mult))
            vmark(v, "zpT")
        blk.vector(_)

        DEF = 2
        x4 = x_in[:, :].rearrange("(t p) (b c) -> p t b c", p=128, c=NHF)
        out4 = out_d[:, :].rearrange("(t p) (b c) -> p t b c", p=128, c=NHF)

        def res_ap(t4, t, p, rp):
            b0 = 4 * rp + p
            return t4[:, t, b0:b0 + 3:2, :]

        for p in range(2):
            ag_out = ag_out1 if p == 0 else ag_out2
            for rp in range(4):
                g = p * 4 + rp
                slot = g % 2

                def _(sync, g=g, rp=rp, p=p, slot=slot, ag_out=ag_out):
                    if g == 0:
                        wait(sync, "ag_in2")
                    if g == 1:
                        wait(sync, f"xnorm{NT-1}")
                    if g >= 2:
                        wait(sync, f"mmk_{g-2}_{NT-1}")
                    wait(sync, "ag1" if p == 0 else "ag2")
                    for s in range(2):
                        for khal in range(2):
                            qM[slot].inc(sync.dma_start(
                                out=mt2s[slot][:, khal * (KC // 2):(khal + 1) * (KC // 2), s, :],
                                in_=ag_out[2 * rp + s, :,
                                           khal * (KC // 2):(khal + 1) * (KC // 2), :]), 16)
                    done(f"mtload{g}", qM[slot])
                blk.sync(_)

                for t in range(NT):
                    it = g * NT + t

                    def _(t_, g=g, rp=rp, p=p, t=t, it=it, slot=slot):
                        wait(t_, f"mtload{g}")
                        if it >= 4:
                            wait(t_, f"ev_{it-4}")
                        elif it == 2:
                            wait(t_, "bm01_3")
                        elif it == 3:
                            wait(t_, "bmex_3")
                        NGRP = (KC + 7) // 8
                        for gi in range(NGRP):
                            wait(t_, f"cp_xng{t}_{gi}")
                        itc = it - DEF
                        if itc >= ITD:
                            if itc == ITD:
                                wait(t_, "zpT")
                                wait(t_, "Bx2")
                            pc, rpc = (itc // NT) // 4, (itc // NT) % 4
                            tc = itc % NT
                            cT.inc(t_.matmul(
                                pp[itc % 4][:, 0:NSL],
                                zpT[:, tc * 128:(tc + 1) * 128],
                                Bx2[0:R + 1, pc * 2048 + rpc * NSL:
                                    pc * 2048 + (rpc + 1) * NSL],
                                start=False, stop=True))
                            done(f"mm_{itc}", cT)
                        ppt = pp[it % 4]
                        for kc in range(KC):
                            ins = t_.matmul(ppt[:, 0:NSL],
                                            xnT[:, kc, t * 128:(t + 1) * 128],
                                            mt2s[slot][:, kc, :, :]
                                            .rearrange("p s n -> p (s n)"),
                                            start=(kc == 0),
                                            stop=(it < ITD and kc == KC - 1))
                        cT.inc(ins)
                        done(f"mmk_{g}_{t}", cT)
                        if it < ITD:
                            done(f"mm_{it}", cT)
                        if it == NIT - 1:
                            for itc2 in range(NIT - DEF, NIT):
                                pc, rpc = (itc2 // NT) // 4, (itc2 // NT) % 4
                                tc = itc2 % NT
                                cT.inc(t_.matmul(
                                    pp[itc2 % 4][:, 0:NSL],
                                    zpT[:, tc * 128:(tc + 1) * 128],
                                    Bx2[0:R + 1, pc * 2048 + rpc * NSL:
                                        pc * 2048 + (rpc + 1) * NSL],
                                    start=False, stop=True))
                                done(f"mm_{itc2}", cT)
                    blk.tensor(_)

                    ready = [it - DEF] if it >= DEF else []
                    for itc in ready:
                        pc, rpc = (itc // NT) // 4, (itc // NT) % 4
                        tc = itc % NT

                        def _(v, pc=pc, rpc=rpc, tc=tc, itc=itc):
                            wait(v, f"mm_{itc}")
                            wait(v, f"xr_{itc}")
                            if itc < ITD:
                                cV.inc(v.tensor_tensor(stg2[:, itc, :],
                                                       pp[itc % 4][:, 0:NSL],
                                                       xrv[:, itc % 2, :], ALU.add))
                            else:
                                if itc - 3 >= ITD:
                                    wait(v, f"st_{itc-3}")
                                cV.inc(v.tensor_tensor(stg[:, itc % 3, :],
                                                       pp[itc % 4][:, 0:NSL],
                                                       xrv[:, itc % 2, :], ALU.add))
                            vmark(v, f"ev_{itc}")
                        blk.vector(_)

                        if itc >= ITD:
                            def _(gp_, pc=pc, rpc=rpc, tc=tc, itc=itc):
                                wait(gp_, f"ev_{itc}")
                                sx = sS[itc % 3]
                                sx.inc(gp_.dma_start(
                                    out=res_ap(out4, tc, pc, rpc),
                                    in_=stg[:, itc % 3, :]
                                    .rearrange("p (s n) -> p s n", s=2)), 16)
                                done(f"st_{itc}", sx)
                            blk.gpsimd(_)

                    def _(gp_, g=g, rp=rp, p=p, t=t, it=it):
                        if it < 2:
                            wait(gp_, "red1_3_3")
                        else:
                            wait(gp_, f"ev_{it-2}")
                        qR[it % 2].inc(gp_.dma_start(
                            out=xrv[:, it % 2, :].rearrange("p (s n) -> p s n", s=2),
                            in_=res_ap(x4, t, p, rp)), 16)
                        done(f"xr_{it}", qR[it % 2])
                    blk.gpsimd(_)

        for itc in range(NIT - DEF, NIT):
            pc, rpc = (itc // NT) // 4, (itc // NT) % 4
            tc = itc % NT

            def _(v, pc=pc, rpc=rpc, tc=tc, itc=itc):
                wait(v, f"mm_{itc}")
                wait(v, f"xr_{itc}")
                if itc - 3 >= ITD:
                    wait(v, f"st_{itc-3}")
                cV.inc(v.tensor_tensor(stg[:, itc % 3, :],
                                       pp[itc % 4][:, 0:NSL],
                                       xrv[:, itc % 2, :], ALU.add))
                vmark(v, f"ev_{itc}")
            blk.vector(_)

            def _(gp_, pc=pc, rpc=rpc, tc=tc, itc=itc):
                wait(gp_, f"ev_{itc}")
                sx = sS[itc % 3]
                sx.inc(gp_.dma_start(
                    out=res_ap(out4, tc, pc, rpc),
                    in_=stg[:, itc % 3, :]
                    .rearrange("p (s n) -> p s n", s=2)), 16)
                done(f"st_{itc}", sx)
            blk.gpsimd(_)

        for itd in range(ITD):
            pc, rpc, tc = 0, 0, itd

            def _(t_, itd=itd, pc=pc, rpc=rpc, tc=tc):
                if itd == 0:
                    for k in range(1, 5):
                        wait(t_, f"ev_{NIT-k}")
                    wait(t_, "zpT")
                    wait(t_, "Bx2")
                else:
                    wait(t_, f"dev_{itd-1}")
                cT.inc(t_.matmul(pp[itd % 4][:, 0:NSL],
                                 zpT[:, tc * 128:(tc + 1) * 128],
                                 Bx2[0:R + 1, pc * 2048 + rpc * NSL:
                                     pc * 2048 + (rpc + 1) * NSL],
                                 start=True, stop=True))
                done(f"dmm_{itd}", cT)
            blk.tensor(_)

            def _(v, itd=itd):
                wait(v, f"dmm_{itd}")
                cV.inc(v.tensor_tensor(stg2[:, itd, :], stg2[:, itd, :],
                                       pp[itd % 4][:, 0:NSL], ALU.add))
                vmark(v, f"dev_{itd}")
            blk.vector(_)

            def _(gp_, itd=itd, pc=pc, rpc=rpc, tc=tc):
                wait(gp_, f"dev_{itd}")
                sx = sS[itd % 3]
                sx.inc(gp_.dma_start(
                    out=res_ap(out4, tc, pc, rpc),
                    in_=stg2[:, itd, :].rearrange("p (s n) -> p s n", s=2)), 16)
                done(f"dst_{itd}", sx)
            blk.gpsimd(_)

        def _(g):
            for scnt in sS:
                g.wait_ge(scnt.sem, scnt.n)
        blk.gpsimd(_)

    return nc


def shard_inputs(data, mask, weight, bias, W, b, A, Bmat, LiMEs, TPC, D):
    xflat = np.ascontiguousarray(np.asarray(data, np.float32).reshape(-1, D))
    mask = np.asarray(mask, np.int32)
    weight = np.asarray(weight, np.float32)
    W = np.asarray(W, np.float32)
    NSL = D // NCORES
    identb = np.eye(128, dtype=ml_dtypes.bfloat16)
    identf = np.eye(128, dtype=np.float32)
    E32 = np.zeros((128, 4), dtype=ml_dtypes.bfloat16)
    for n in range(128):
        E32[n, n // 32] = 1
    Emat = np.ascontiguousarray(E32.T)
    in_maps = []
    for c in range(NCORES):
        in_maps.append({
            "x": np.ascontiguousarray(xflat[c * TPC:(c + 1) * TPC]),
            "mask_sl": np.ascontiguousarray(mask[c * NSL:(c + 1) * NSL]),
            "w_sl": np.ascontiguousarray(weight[c * NSL:(c + 1) * NSL]),
            "Wf_sl": np.ascontiguousarray(W[c * NSL:(c + 1) * NSL]),
            "W8": np.ascontiguousarray(W[0:E]),
            "biasrow": np.asarray(bias, np.float32).reshape(1, D),
            "brow": np.asarray(b, np.float32).reshape(1, D),
            "A": np.asarray(A, np.float32),
            "Bmat": np.asarray(Bmat, np.float32),
            "LiMEs": np.asarray(LiMEs, np.float32),
            "identb": identb,
            "identf": identf,
            "E32": E32,
            "Emat": Emat,
        })
    return in_maps


from concourse.bass_utils import run_bass_kernel_spmd

TPC_REAL = 1024
D_REAL = 4096


def kernel(L0_attn_data, L0_attn_mask, L0_attn_weight, L0_attn_bias,
           L0_ffn_W, L0_ffn_b, L0_ffn_A, L0_ffn_Bmat, L0_ffn_LiMEs):
    TPC, D = TPC_REAL, D_REAL
    nc = build_program(TPC, D, detect_races=True)
    in_maps = shard_inputs(L0_attn_data, L0_attn_mask, L0_attn_weight,
                           L0_attn_bias, L0_ffn_W, L0_ffn_b, L0_ffn_A,
                           L0_ffn_Bmat, L0_ffn_LiMEs, TPC, D)
    res = run_bass_kernel_spmd(nc, in_maps, core_ids=list(range(NCORES)))
    out = np.concatenate([np.asarray(r["out"]) for r in res.results], axis=0)
    return out.reshape(4, 2048, D).astype(np.float32, copy=False)
